# version 07
# baseline (speedup 1.0000x reference)
import math
import numpy as np
import ml_dtypes

import concourse.bass as bass
import concourse.mybir as mybir
from concourse.bass_utils import run_bass_kernel_spmd
from contextlib import ExitStack

F32 = mybir.dt.float32
BF16 = mybir.dt.bfloat16
I32 = mybir.dt.int32
AF = mybir.ActivationFunctionType
ALU = mybir.AluOpType
AX = mybir.AxisListType

bf16 = ml_dtypes.bfloat16

NCORES = 8
D = 1024
CC = 8
TFULL = 4096
TOWN = 2048
NHP = 8
NQI = 4
NKI = 32
DFF = 4096
NE = 8
FT = 32
WSUB = 1920
WWIN = 4 * WSUB
EPS = 1e-5

GSEG = {0: [0, 3072, 1024, 2048], 1: [1024, 2048, 0, 3072]}

DBG = False
ARMS = True
DEQ = True
UNITS5 = True
N_HP5 = 8
MASK5 = True
UNITLVL = 3
EQNOPS = False



def prep_core_inputs(I, c):
    b, r = divmod(c, 2)
    segs = GSEG[r]
    perm = np.concatenate([np.arange(g, g + 1024) for g in segs])

    d = {}
    d["xt"] = np.ascontiguousarray(I["L0_attn_x"][b].T[:, perm])
    d["posrow"] = perm.astype(np.float32)[None, :]
    kk = np.arange(128)[:, None]
    jj = np.arange(WSUB)[None, :]
    dwin = np.empty((2, 128, WWIN), np.float32)
    for half in range(2):
        for ks in range(4):
            dwin[half, :, ks * WSUB:(ks + 1) * WSUB] = np.abs(
                kk + segs[ks] - segs[half] + 896 - jj).astype(np.float32)
    d["dwin"] = dwin
    qT = np.ascontiguousarray(I["L0_attn_qkv_w"].T)
    t = qT.reshape(CC, 128, 3, NHP, 128)
    d["qkvw0"] = np.ascontiguousarray(t.transpose(3, 2, 0, 1, 4))
    ow = np.ascontiguousarray(I["L0_attn_out_w"].T)
    d["outw0"] = np.ascontiguousarray(
        ow.reshape(CC, 128, CC, 128).transpose(0, 2, 1, 3))
    d["pa"] = np.stack([I["L0_attn_bias_p"].ravel(),
                        I["L0_attn_bias_a"].ravel()]).astype(np.float32)
    d["freqs"] = np.tile(I["L0_attn_rope_freqs"].astype(np.float32), 4)[:, None]
    d["wr"] = np.ascontiguousarray(I["L0_ffn_Wr"].T)
    d["br"] = I["L0_ffn_br"].astype(np.float32)[None, :]
    W1T = I["L0_ffn_W1"].transpose(0, 2, 1)
    d["w1p"] = np.ascontiguousarray(
        W1T.reshape(NE, CC, 128, 16, 256).transpose(0, 3, 2, 1, 4))
    d["b1p"] = np.ascontiguousarray(
        I["L0_ffn_b1"].reshape(NE, FT, 128).transpose(0, 2, 1))
    W2T = I["L0_ffn_W2"].transpose(0, 2, 1)
    d["w2p"] = np.ascontiguousarray(
        W2T.reshape(NE, FT, 128, CC, 128).transpose(0, 3, 1, 2, 4))
    d["b2p"] = np.ascontiguousarray(
        I["L0_ffn_b2"].reshape(NE, CC, 128).transpose(0, 2, 1))
    wq = I["L1_attn_W_Q"].reshape(D, D)
    wk = I["L1_attn_W_K"].reshape(D, D)
    wv = I["L1_attn_W_V"].reshape(D, D)
    qkv1 = np.stack([wq, wk, wv])
    t = qkv1.reshape(3, CC, 128, NHP, 128)
    d["qkvw1"] = np.ascontiguousarray(t.transpose(3, 0, 1, 2, 4))
    ow1 = I["L1_attn_W_out"].transpose(1, 0, 2).reshape(D, D)
    d["outw1"] = np.ascontiguousarray(
        ow1.reshape(CC, 128, CC, 128).transpose(0, 2, 1, 3))
    d["boutp"] = np.ascontiguousarray(
        I["L1_attn_b_out"].reshape(CC, 128).T)
    d["lgam"] = np.ascontiguousarray(I["L1_ffn_ln_gamma"].reshape(CC, 128).T)
    d["lbet"] = np.ascontiguousarray(I["L1_ffn_ln_beta"].reshape(CC, 128).T)
    d["fc1q"] = I["L1_ffn_fc1_q"]
    d["fc1s"] = I["L1_ffn_fc1_scales"]
    d["fc1bp"] = np.ascontiguousarray(
        I["L1_ffn_fc1_bias"].reshape(FT, 128).T)
    d["fc2q"] = I["L1_ffn_fc2_q"]
    d["fc2s"] = I["L1_ffn_fc2_scales"]
    d["fc2bp"] = np.ascontiguousarray(
        I["L1_ffn_fc2_bias"].reshape(CC, 128).T)
    d["params"] = np.array([[r, 0, 0, 0]], np.int32)
    return d


def assemble_output(results):
    out = np.empty((4, 4096, 1024), np.float32)
    for c in range(NCORES):
        b, r = divmod(c, 2)
        sh = np.asarray(results[c]["out"], np.float32)
        for si in range(2):
            g = GSEG[r][si]
            out[b, g:g + 1024, :] = sh[:, si * 1024:(si + 1) * 1024].T
    return out



ENGS = ("tensor", "scalar", "vector", "gpsimd", "sync")


class MK:
    def __init__(self, nc, es):
        self.nc = nc
        self.es = es
        self.ops = {e: [] for e in ENGS}
        self.cnt = {}
        self.sem = {}
        for name in ("tt", "sc", "ve", "gg", "hw", "gw", "cc"):
            self.sem[name] = es.enter_context(nc.semaphore("s_" + name))
            self.cnt[name] = 0
        self.dbg_outs = []

    def em(self, eng, fn):
        self.ops[eng].append(fn)

    def wait(self, eng, ctr, val):
        assert val <= self.cnt[ctr], f"wait on future {ctr}: {val} > {self.cnt[ctr]}"
        sem = self.sem[ctr]
        self.em(eng, lambda E, s=sem, v=val: E.wait_ge(s, v))

    def op(self, eng, meth, *args, ctr=None, inck=1, **kw):
        sem = None
        if ctr is not None:
            self.cnt[ctr] += inck
            sem = self.sem[ctr]
        def f(E, meth=meth, args=args, kw=kw, sem=sem, inck=inck):
            ins = getattr(E, meth)(*args, **kw)
            if sem is not None:
                ins.then_inc(sem, inck)
        self.em(eng, f)
        return self.cnt[ctr] if ctr is not None else None

    def nop_inc(self, eng, ctr, inck=1):
        sem = self.sem[ctr]
        for _ in range(inck):
            self.cnt[ctr] += 1
            self.em(eng, lambda E, s=sem: E.nop().then_inc(s, 1))
        return self.cnt[ctr]

    def sem_pad(self, eng, ctr, target):
        cur = self.cnt[ctr]
        assert target >= cur
        if target > cur:
            sem = self.sem[ctr]
            d = target - cur
            self.em(eng, lambda E, s=sem, d=d: E.sem_inc(s, d))
            self.cnt[ctr] = target

    def dma(self, queue, out, in_, ctr=None, **kw):
        ctr = ctr or ("hw" if queue == "sync" else "gw")
        self.cnt[ctr] += 16
        sem = self.sem[ctr]
        def f(E, out=out, in_=in_, kw=kw, sem=sem):
            E.dma_start(out=out, in_=in_, **kw).then_inc(sem, 16)
        self.em(queue, f)
        return self.cnt[ctr]

    def run_block(self, block):
        with self.nc.allow_low_precision(reason="deliberate bf16 stream casts"):
            for eng in ENGS:
                fns = self.ops[eng]
                getattr(block, eng)(lambda E, fns=fns: [f(E) for f in fns])



def build_program(dbg=False, phases=("p0", "p1", "p2", "p3", "p4", "p5", "p6")):
    nc = bass.Bass("TRN2", target_bir_lowering=False, detect_race_conditions=False)

    din = {}
    def dri(name, shape, dt=F32):
        din[name] = nc.dram_tensor(name, list(shape), dt, kind="ExternalInput")
        return din[name]
    dri("xt", (D, TFULL)); dri("posrow", (1, TFULL)); dri("dwin", (2, 128, WWIN))
    dri("qkvw0", (NHP, 3, CC, 128, 128)); dri("outw0", (CC, CC, 128, 128))
    dri("pa", (2, 16)); dri("freqs", (128, 1))
    dri("wr", (D, 8)); dri("br", (1, 8))
    dri("w1p", (NE, 16, 128, CC, 256)); dri("b1p", (NE, 128, FT))
    dri("w2p", (NE, CC, FT, 128, 128)); dri("b2p", (NE, 128, CC))
    dri("qkvw1", (NHP, 3, CC, 128, 128)); dri("outw1", (CC, CC, 128, 128))
    dri("boutp", (128, CC)); dri("lgam", (128, CC)); dri("lbet", (128, CC))
    dri("fc1q", (D, DFF), I32); dri("fc1s", (8, DFF)); dri("fc1bp", (128, FT))
    dri("fc2q", (DFF, D), I32); dri("fc2s", (32, D)); dri("fc2bp", (128, CC))
    dri("params", (1, 4), I32)

    out_t = nc.dram_tensor("out", [D, TOWN], F32, kind="ExternalOutput")

    d_z0 = nc.dram_tensor("d_z0", [NHP, 128, TOWN], BF16)
    d_s0 = nc.dram_tensor("d_s0", [CC, 128, TOWN], F32)
    d_y = nc.dram_tensor("d_y", [CC, 128, TOWN], F32)
    d_x1 = nc.dram_tensor("d_x1", [CC, 128, TOWN], F32)
    d_s1 = nc.dram_tensor("d_s1", [CC, 128, TOWN], F32)
    d_lg = nc.dram_tensor("d_lg", [8, TOWN], F32)
    d_z1 = nc.dram_tensor("d_z1", [NHP, 128, TOWN], BF16)
    d_w1bf = nc.dram_tensor("d_w1bf", [CC, 128, DFF], BF16)
    d_w2bf = nc.dram_tensor("d_w2bf", [CC, 128, DFF], BF16)
    cc_in = nc.dram_tensor("cc_in", [128, CC * TOWN], BF16)
    cc_out = nc.dram_tensor("cc_out", [2, 128, CC * TOWN], BF16)

    dbg_list = []
    def dbg_out(name, shape, dt=F32):
        t = nc.dram_tensor(name, list(shape), dt, kind="ExternalOutput")
        dbg_list.append(name)
        return t

    es = ExitStack()
    with es:
        XNo = es.enter_context(nc.sbuf_tensor("XNo", [128, CC * TOWN], BF16))
        XNs = es.enter_context(nc.sbuf_tensor("XNs", [128, CC * TOWN], BF16))
        ARB = 120 * 1024
        AR = es.enter_context(nc.sbuf_tensor("AR", [128, ARB // 2], BF16))

        def reg(off_kb, nelem, dt=F32, parts=128):
            off = int(off_kb * 1024)
            assert off % 64 == 0
            nbytes = nelem * (4 if dt == F32 else 2)
            assert off + nbytes <= ARB, f"arena overflow @{off_kb}KB + {nbytes}"
            a = AR[0:parts, off // 2:(off + nbytes) // 2]
            if dt != BF16:
                a = a.bitcast(dt)
            return a

        sbs = lambda n, sh, dt=F32: es.enter_context(nc.sbuf_tensor(n, list(sh), dt))
        identb = sbs("identb", (128, 128), BF16)
        halfsel = sbs("halfsel", (2, 128), BF16)
        onesb = sbs("onesb", (128, 2), BF16)
        onescf = sbs("onescf", (128, 2))
        ones1f = sbs("ones1f", (1, 128))
        signcol = sbs("signcol", (128, 2))
        freqcol = sbs("freqcol", (128, 2))
        acol = sbs("acol", (128, 16))
        pcoln = sbs("pcoln", (128, 16))
        brcol = sbs("brcol", (8, 2))
        wrsb = sbs("wrsb", (128, CC * 8))
        b1ps = sbs("b1ps", (128, NE * FT))
        b2ps = sbs("b2ps", (128, NE * CC))
        fc1bs = sbs("fc1bs", (128, FT))
        fc2bs = sbs("fc2bs", (128, CC))
        boutس = None
        bouts = sbs("bouts", (128, CC))
        lgams = sbs("lgams", (128, CC))
        lbets = sbs("lbets", (128, CC))
        paramsb = sbs("paramsb", (1, 4), I32)
        masksL1 = sbs("masksL1", (128, 4 * 512), BF16)
        zerob = sbs("zerob", (128, 128), BF16)

        PSB = [es.enter_context(nc.psum_tensor(f"pb{i}", [128, 512], F32)) for i in range(8)]

        hs_np = np.zeros((2, 128), ml_dtypes.bfloat16)
        hs_np[0, 0:64] = 1.0
        hs_np[1, 64:128] = 1.0
        d_halfsel = nc.inline_tensor(hs_np, "c_halfsel")
        sg_np = np.where((np.arange(128) % 64) < 32, -1.0, 1.0).astype(np.float32)[:, None]
        d_signcol = nc.inline_tensor(np.ascontiguousarray(np.tile(sg_np, (1, 2))), "c_signcol")

        mk = MK(nc, es)
        block = es.enter_context(nc.Block())

        def xn_sl(cc, t0, n):
            assert (t0 < TOWN) == (t0 + n <= TOWN), "xn slice crosses own/sib boundary"
            if t0 < TOWN:
                return XNo[:, cc * TOWN + t0: cc * TOWN + t0 + n]
            return XNs[:, cc * TOWN + (t0 - TOWN): cc * TOWN + (t0 - TOWN) + n]

        posb = reg(0, 4096)
        angs = reg(16, 4096)
        csc = reg(32, 4096, BF16)
        css = reg(40, 4096, BF16)
        xf32 = [reg(48, CC * 512), reg(64, CC * 512)]
        mub = reg(80, 512); rb = reg(82, 512); murb = reg(84, 512)
        scr1 = reg(86, 512); scr2 = reg(88, 512)
        st2 = reg(90, 1024)
        scr_k = reg(94, 4096)
        x2st = [reg(110, 512, BF16), reg(111, 512, BF16)]
        se8n = reg(112, 16)
        lgs = reg(113, 512)

        def phase0():
            mk.op("vector", "memset", identb[:, :], 0.0, ctr="ve")
            mk.op("vector", "memset", onesb[:, :], 1.0, ctr="ve")
            mk.op("vector", "memset", onescf[:, :], 1.0, ctr="ve")
            mk.op("vector", "memset", ones1f[:, :], 1.0, ctr="ve")

            mk.op("vector", "memset", masksL1[:, :], 0.0, ctr="ve")
            v_konst = mk.cnt["ve"]
            mk.wait("gpsimd", "ve", v_konst)
            mk.op("gpsimd", "affine_select", identb[:, :], identb[:, :], [[-1, 128]],
                  ALU.not_equal, 1.0, base=0, channel_multiplier=1, ctr="gg")
            mk.dma("sync", halfsel[:, :], d_halfsel.ap())
            mk.dma("sync", signcol[:, :], d_signcol.ap())
            for pi, off in enumerate((0, 128, 256, 384)):
                mk.op("gpsimd", "affine_select", masksL1[:, pi * 512:(pi + 1) * 512],
                      masksL1[:, pi * 512:(pi + 1) * 512], [[1, 512]], ALU.is_ge,
                      -1e6, base=-off, channel_multiplier=-1, ctr="gg")
            g_konst = mk.cnt["gg"]

            mk.dma("sync", brcol[0:8, 0:1], bass.AP(din["br"], 0, [[1, 8], [1, 1]]))
            mk.dma("sync", wrsb[:, :].rearrange("p (cc e) -> p cc e", cc=CC),
                   din["wr"].ap().rearrange("(cc p) e -> p cc e", p=128))
            mk.dma("sync", b1ps[:, :].rearrange("p (e f) -> p e f", e=NE),
                   din["b1p"].ap().rearrange("e p f -> p e f"))
            mk.dma("sync", b2ps[:, :].rearrange("p (e c) -> p e c", e=NE),
                   din["b2p"].ap().rearrange("e p c -> p e c"))
            mk.dma("sync", fc1bs[:, :], din["fc1bp"].ap())
            mk.dma("sync", fc2bs[:, :], din["fc2bp"].ap())
            mk.dma("sync", bouts[:, :], din["boutp"].ap())
            mk.dma("sync", lgams[:, :], din["lgam"].ap())
            mk.dma("sync", lbets[:, :], din["lbet"].ap())
            mk.dma("sync", paramsb[:, :], din["params"].ap())
            mk.dma("sync", freqcol[:, 0:1], din["freqs"].ap())
            h_small = mk.cnt["hw"]
            mk.h_small = h_small

            mk.dma("gpsimd", acol[:, :], din["pa"].ap()[1:2, :].partition_broadcast(128))
            mk.dma("gpsimd", pcoln[:, :], din["pa"].ap()[0:1, :].partition_broadcast(128))
            mk.dma("gpsimd", posb[:, :], din["posrow"].ap().partition_broadcast(128))
            g_pa = mk.cnt["gw"]
            mk.dma("gpsimd", XNo[:, :].rearrange("p (c t) -> p c t", c=CC),
                   din["xt"].ap().rearrange("(c p) t -> p c t", p=128)[:, :, 0:TOWN])
            mk.dma("gpsimd", XNs[:, :].rearrange("p (c t) -> p c t", c=CC),
                   din["xt"].ap().rearrange("(c p) t -> p c t", p=128)[:, :, TOWN:TFULL])
            g_xload = mk.cnt["gw"]
            mk.g_xload = g_xload

            TWO_PI = 6.283185307179586
            INV2PI = 1.0 / TWO_PI
            MAGIC = 1.5 * 2.0 ** 23
            h1 = float(np.float32(round(TWO_PI * 2 ** 9) / 2 ** 9))
            rem = TWO_PI - h1
            h2 = float(np.float32(rem))
            h3 = float(TWO_PI - h1 - float(np.float32(rem)))
            mk.wait("vector", "gw", g_pa)
            mk.wait("vector", "hw", h_small)
            mk.op("vector", "tensor_single_scalar", acol[:, :], acol[:, :], 0.01, ALU.max, ctr="ve")
            mk.op("vector", "tensor_single_scalar", pcoln[:, :], pcoln[:, :], 0.01, ALU.max, ctr="ve")
            mk.op("vector", "tensor_single_scalar", pcoln[:, :], pcoln[:, :], -1.0, ALU.mult, ctr="ve")
            mk.v_pa = mk.cnt["ve"]
            for which in range(2):
                if which == 1:
                    mk.wait("vector", "sc", mk.cnt["sc"])
                mk.op("vector", "tensor_single_scalar", angs[:, :], posb[:, :],
                      freqcol[:, 0:1], ALU.mult, ctr="ve")
                if which == 0:
                    mk.op("vector", "tensor_single_scalar", angs[:, :], angs[:, :],
                          float(np.pi / 2), ALU.add, ctr="ve")
                mk.op("vector", "tensor_scalar", scr_k[:, :], angs[:, :],
                      INV2PI, MAGIC, ALU.mult, ALU.add, ctr="ve")
                mk.op("vector", "tensor_single_scalar", scr_k[:, :], scr_k[:, :],
                      MAGIC, ALU.subtract, ctr="ve")
                for hh in (h1, h2, h3):
                    mk.op("vector", "scalar_tensor_tensor", angs[:, :], scr_k[:, :], -hh,
                          angs[:, :], ALU.mult, ALU.add, ctr="ve")
                mk.op("vector", "tensor_scalar", angs[:, :], angs[:, :],
                      float(-np.pi), float(np.pi), ALU.max, ALU.min, ctr="ve")
                v_ang = mk.cnt["ve"]
                mk.wait("scalar", "ve", v_ang)
                if which == 0:
                    mk.op("scalar", "activation", csc[:, :], angs[:, :], AF.Sin, ctr="sc")
                else:
                    mk.op("scalar", "activation", angs[:, :], angs[:, :], AF.Sin, ctr="sc")
                    mk.wait("vector", "sc", mk.cnt["sc"])
                    mk.op("vector", "tensor_single_scalar", css[:, :], angs[:, :],
                          signcol[:, 0:1], ALU.mult, ctr="ve")
            mk.cs_done_sc = mk.cnt["sc"]
            mk.cs_done_ve = mk.cnt["ve"]

            SUMB, SQB, LGB, SEB = PSB[0], PSB[1], PSB[2], PSB[3]
            mk.wait("tensor", "hw", h_small)
            for cc in range(CC):
                if cc == 0:
                    mk.wait("tensor", "ve", v_konst)
                mk.op("tensor", "matmul", SEB[0:8, 0:1], wrsb[:, cc * 8:(cc + 1) * 8],
                      onescf[:, 0:1], start=(cc == 0),
                      stop=(cc == CC - 1), ctr="tt")
            t_se = mk.cnt["tt"]
            mk.wait("scalar", "tt", t_se)
            mk.op("scalar", "mul", se8n[0:8, 0:1], SEB[0:8, 0:1], -1.0, ctr="sc")
            s_se8 = mk.cnt["sc"]

            ev_sq = {}
            mm_sq = {}
            s_statdone = 0
            lg_done = {}
            lg_dma = {}
            lg_fold_v = {}
            lgst_done = {}
            for tt in range(8):
                own = tt < 4
                if own:
                    stg = xf32[tt % 2]
                    if tt >= 2:
                        mk.wait("sync", "tt", lg_done[tt - 2])
                    mk.dma("sync", stg[:, :].rearrange("p (c t) -> p c t", c=CC),
                           din["xt"].ap().rearrange("(c p) t -> p c t", p=128)
                           [:, :, tt * 512:(tt + 1) * 512])
                    h_x32 = mk.cnt["hw"]
                for cc in range(CC):
                    xs = xn_sl(cc, tt * 512, 512)
                    k = (tt * CC + cc)
                    if tt == 0 and cc == 0:
                        mk.wait("vector", "gw", g_xload)
                    if k >= 2:
                        mk.wait("vector", "tt", mm_sq[k - 2])
                    mk.op("vector", "tensor_tensor", x2st[k % 2][:, :], xs, xs,
                          ALU.mult, ctr="ve")
                    ev_sq[k] = mk.cnt["ve"]
                    if cc == 0:
                        mk.wait("tensor", "sc", s_statdone)
                        mk.wait("tensor", "gw", g_xload)
                    mk.wait("tensor", "ve", ev_sq[k])
                    mk.op("tensor", "matmul", SUMB[0:1, :], onesb[:, 0:1], xs,
                          start=(cc == 0), stop=(cc == CC - 1), ctr="tt")
                    mk.op("tensor", "matmul", SQB[0:1, :], onesb[:, 0:1], x2st[k % 2][:, :],
                          start=(cc == 0), stop=(cc == CC - 1), ctr="tt")
                    mm_sq[k] = mk.cnt["tt"]
                t_stats = mk.cnt["tt"]
                mk.wait("scalar", "tt", t_stats)
                mk.op("scalar", "copy", st2[0:1, 0:512], SUMB[0:1, :], ctr="sc")
                mk.op("scalar", "copy", st2[0:1, 512:1024], SQB[0:1, :], ctr="sc")
                s_ev = mk.cnt["sc"]
                mk.wait("tensor", "sc", s_ev)
                mk.op("tensor", "matmul", SUMB[:, :], ones1f[:, :], st2[0:1, 0:512],
                      start=True, stop=True, ctr="tt")
                mk.op("tensor", "matmul", SQB[:, :], ones1f[:, :], st2[0:1, 512:1024],
                      start=True, stop=True, ctr="tt")
                t_bc = mk.cnt["tt"]
                mk.wait("scalar", "tt", t_bc)
                if tt > 0:
                    mk.wait("scalar", "ve", v_xnap)
                mk.op("scalar", "mul", mub[:, :], SUMB[:, :], 1.0 / D, ctr="sc")
                mk.op("scalar", "mul", scr1[:, :], SQB[:, :], 1.0 / D, ctr="sc")
                s_bc = mk.cnt["sc"]
                mk.wait("vector", "sc", s_bc)
                mk.op("vector", "tensor_tensor", scr2[:, :], mub[:, :], mub[:, :], ALU.mult, ctr="ve")
                mk.op("vector", "tensor_sub", scr1[:, :], scr1[:, :], scr2[:, :], ctr="ve")
                mk.op("vector", "tensor_single_scalar", scr1[:, :], scr1[:, :], EPS, ALU.add, ctr="ve")
                mk.op("vector", "reciprocal", scr2[:, :], scr1[:, :], ctr="ve")
                v_rc = mk.cnt["ve"]
                mk.wait("scalar", "ve", v_rc)
                mk.op("scalar", "activation", rb[:, :], scr2[:, :], AF.Sqrt, ctr="sc")
                s_rb = mk.cnt["sc"]
                s_statdone = s_rb
                mk.wait("vector", "sc", s_rb)
                mk.op("vector", "tensor_tensor", murb[:, :], mub[:, :], rb[:, :], ALU.mult, ctr="ve")
                for cc in range(CC):
                    xs = xn_sl(cc, tt * 512, 512)
                    mk.op("vector", "tensor_tensor", xs, xs, rb[:, :], ALU.mult, ctr="ve")
                    mk.op("vector", "tensor_sub", xs, xs, murb[:, :], ctr="ve")
                v_xnap = mk.cnt["ve"]

                if own:
                    S32, Q32 = PSB[4], PSB[5]
                    mk.wait("tensor", "hw", h_x32)
                    if tt >= 1:
                        mk.wait("tensor", "ve", lg_fold_v[tt - 1])
                    x2f = scr_k[:, 0:512]
                    for cc in range(CC):
                        sl = slice(cc * 512, (cc + 1) * 512)
                        mk.wait("vector", "hw", h_x32)
                        if cc == 0 and tt >= 1:
                            mk.wait("vector", "tt", lgst_done[tt - 1])
                        if cc >= 1:
                            mk.wait("vector", "tt", mk.cnt["tt"])
                        mk.op("vector", "tensor_tensor", x2f[:, :], stg[:, sl], stg[:, sl],
                              ALU.mult, ctr="ve")
                        v_x2 = mk.cnt["ve"]
                        mk.op("tensor", "matmul", LGB[0:8, :],
                              wrsb[:, cc * 8:(cc + 1) * 8], stg[:, sl],
                              start=(cc == 0), stop=(cc == CC - 1), ctr="tt")
                        mk.wait("tensor", "ve", v_x2)
                        mk.op("tensor", "matmul", S32[0:1, :], onescf[:, 0:1], stg[:, sl],
                              start=(cc == 0), stop=(cc == CC - 1), ctr="tt")
                        mk.op("tensor", "matmul", Q32[0:1, :], onescf[:, 0:1], x2f[:, :],
                              start=(cc == 0), stop=(cc == CC - 1), ctr="tt")
                    lg_done[tt] = mk.cnt["tt"]
                    lgst_done[tt] = mk.cnt["tt"]
                    mk.wait("scalar", "tt", lg_done[tt])
                    mk.op("scalar", "copy", scr_k[0:1, 1024:1536], S32[0:1, :], ctr="sc")
                    mk.op("scalar", "copy", scr_k[0:1, 1536:2048], Q32[0:1, :], ctr="sc")
                    s_e32 = mk.cnt["sc"]
                    mk.wait("tensor", "sc", s_e32)
                    mk.op("tensor", "matmul", S32[0:8, :], ones1f[:, 0:8], scr_k[0:1, 1024:1536],
                          start=True, stop=True, ctr="tt")
                    mk.op("tensor", "matmul", Q32[0:8, :], ones1f[:, 0:8], scr_k[0:1, 1536:2048],
                          start=True, stop=True, ctr="tt")
                    t_b32 = mk.cnt["tt"]
                    mu32 = lgs[0:8, 0:512]
                    mk.wait("vector", "tt", t_b32)
                    mk.op("vector", "tensor_single_scalar", scr1[0:8, :], S32[0:8, :],
                          1.0 / D, ALU.mult, ctr="ve")
                    mk.op("vector", "tensor_single_scalar", scr2[0:8, :], Q32[0:8, :],
                          1.0 / D, ALU.mult, ctr="ve")
                    mk.op("vector", "tensor_tensor", x2f[0:8, :], scr1[0:8, :], scr1[0:8, :],
                          ALU.mult, ctr="ve")
                    mk.op("vector", "tensor_sub", scr2[0:8, :], scr2[0:8, :], x2f[0:8, :], ctr="ve")
                    mk.op("vector", "tensor_single_scalar", scr2[0:8, :], scr2[0:8, :],
                          EPS, ALU.add, ctr="ve")
                    mk.op("vector", "reciprocal", x2f[0:8, :], scr2[0:8, :], ctr="ve")
                    v_r32 = mk.cnt["ve"]
                    mk.wait("scalar", "ve", v_r32)
                    mk.op("scalar", "activation", scr2[0:8, :], x2f[0:8, :], AF.Sqrt, ctr="sc")
                    s_r32 = mk.cnt["sc"]
                    mk.wait("vector", "tt", lg_done[tt])
                    mk.wait("vector", "sc", s_r32)
                    if tt == 0:
                        mk.wait("vector", "sc", s_se8)
                    if tt >= 1:
                        mk.wait("vector", "hw", lg_dma[tt - 1])
                    mk.op("vector", "tensor_tensor", scr1[0:8, :], scr1[0:8, :],
                          scr2[0:8, :], ALU.mult, ctr="ve")
                    mk.op("vector", "tensor_tensor", lgs[0:8, :], LGB[0:8, :], scr2[0:8, :],
                          ALU.mult, ctr="ve")
                    mk.op("vector", "scalar_tensor_tensor", lgs[0:8, :], scr1[0:8, :],
                          se8n[0:8, 0:1], lgs[0:8, :], ALU.mult, ALU.add, ctr="ve")
                    mk.op("vector", "tensor_single_scalar", lgs[0:8, :], lgs[0:8, :],
                          brcol[0:8, 0:1], ALU.add, ctr="ve")
                    lg_fold_v[tt] = mk.cnt["ve"]
                    mk.wait("sync", "ve", lg_fold_v[tt])
                    lg_dma[tt] = mk.dma("sync", d_lg.ap()[:, tt * 512:(tt + 1) * 512], lgs[0:8, :])
            mk.h_lg_stored = mk.cnt["hw"]
            mk.v_xn_done = v_xnap
            mk.s_ln_done = s_statdone

        if "p0" in phases:
            phase0()

        QB = reg(48, TOWN, BF16)
        KB = reg(52, TFULL, BF16)
        VTM = reg(60, 2 * TFULL, BF16)
        SW = reg(76, TFULL, BF16)
        WSL = [reg(84, 3072, BF16), reg(90, 3072, BF16)]
        VSL = reg(96, 2048, BF16)
        MW = [reg(100, WWIN, BF16), reg(0, WWIN, BF16)]
        DT = reg(15, 3840)
        ZSB = reg(115, 2048, BF16)

        EXT = [sbs(f"EXT{i}", (128, 512), BF16) for i in range(4)]
        PT = [sbs(f"PT{i}", (128, 512), BF16) for i in range(4)]
        recs = sbs("recs", (128, 512))
        recb = sbs("recb", (128, 512))
        rbcb = sbs("rbcb", (128, 512))

        hs128_np = np.zeros((128, 128), np.float32)
        hs128_np[0, 0:64] = 1.0
        hs128_np[64, 64:128] = 1.0

        d_hs128 = nc.inline_tensor(hs128_np, "c_hs128")
        hs128 = sbs("hs128", (128, 128))
        d_idf8 = nc.inline_tensor(np.eye(8, dtype=np.float32), "c_idf8")
        rs_np = np.zeros((8, 8 * 128), ml_dtypes.bfloat16)
        for _e in range(8):
            rs_np[_e, _e * 128:(_e + 1) * 128] = 1.0
        d_rowsel = nc.inline_tensor(rs_np, "c_rowsel")
        rowsel = sbs("rowsel", (8, 8 * 128), BF16)

        def moff(qih, ki):
            ks, kj = divmod(ki, 8)
            return ks * WSUB + 896 - 128 * kj + 512 * qih

        def phase1():
            mk.dma("sync", hs128[:, :], d_hs128.ap())

            mk.op("vector", "memset", zerob[:, :], 0.0, ctr="ve")
            mk.op("vector", "memset", recb[:, :], 0.0, ctr="ve")
            st1 = dict(u=0, E={}, V={}, Ts={}, Ta={}, meta={},
                       rbc_evac={}, z_evac={}, recip_v={}, denev_s={}, bc_t={},
                       pend=[], zspill={}, hp_first_u={})
            E, V, Ts, Ta = st1["E"], st1["V"], st1["Ts"], st1["Ta"]
            proj_done_t = {}; slab_g = {}; vslab_g = {}
            rope_done = {}; vtm_evac = {}; vtm_mm_done = {}
            mwdone = {}

            def emit_qi_tail_a(qi_g, qpar, ulast):
                if qi_g >= 1:
                    mk.wait("scalar", "ve", st1["recip_v"][qi_g - 1])
                mk.wait("scalar", "tt", Ta[ulast])
                DEN = PSB[6 + qpar]
                mk.op("scalar", "copy", recs[0:1, :], DEN[0:1, :], ctr="sc")
                mk.op("scalar", "copy", recs[64:65, :], DEN[64:65, :], ctr="sc")
                st1["denev_s"][qi_g] = mk.cnt["sc"]
                if qi_g >= 1:
                    mk.wait("vector", "tt", st1["bc_t"][qi_g - 1])
                mk.wait("vector", "sc", st1["denev_s"][qi_g])
                mk.op("vector", "reciprocal", recb[0:1, :], recs[0:1, :], ctr="ve")
                mk.op("vector", "reciprocal", recb[64:65, :], recs[64:65, :], ctr="ve")
                st1["recip_v"][qi_g] = mk.cnt["ve"]
                st1["pend"].append((st1["u"] + 4, qi_g, qpar))

            def emit_qi_tail_b(qi_g, qpar):
                hp, qt = divmod(qi_g, 4)
                DEN = PSB[6 + qpar]
                Z = PSB[4 + qpar]
                mk.wait("tensor", "ve", st1["recip_v"][qi_g])
                mk.op("tensor", "matmul", DEN[:, :], hs128[:, :], recb[:, :],
                      start=True, stop=True, ctr="tt")
                st1["bc_t"][qi_g] = mk.cnt["tt"]
                if qi_g >= 1:
                    mk.wait("scalar", "ve", st1["z_evac"][qi_g - 1])
                mk.wait("scalar", "tt", st1["bc_t"][qi_g])
                mk.op("scalar", "copy", rbcb[:, :], DEN[:, :], ctr="sc")
                st1["rbc_evac"][qi_g] = mk.cnt["sc"]
                mk.wait("vector", "sc", st1["rbc_evac"][qi_g])
                if qt == 0 and hp >= 1:
                    mk.wait("vector", "hw", st1["zspill"][hp - 1])
                mk.op("vector", "tensor_tensor", ZSB[:, qt * 512:(qt + 1) * 512],
                      Z[:, :], rbcb[:, :], ALU.mult, ctr="ve")
                st1["z_evac"][qi_g] = mk.cnt["ve"]
                if qt == 3:
                    mk.wait("sync", "ve", st1["z_evac"][qi_g])
                    st1["zspill"][hp] = mk.dma("sync", d_z0.ap()[hp], ZSB[:, :])

            def emit_denav(v):
                hp, qt, qi_g, qpar, ki, pi = st1["meta"][v]
                DEN = PSB[6 + qpar]
                Z = PSB[4 + qpar]
                mk.wait("tensor", "ve", V[v])
                if ki == 0:
                    if qi_g >= 2:
                        mk.wait("tensor", "sc", st1["rbc_evac"][qi_g - 2])
                        mk.wait("tensor", "ve", st1["z_evac"][qi_g - 2])
                if ki == 0:
                    mk.op("tensor", "matmul", DEN[:, :], zerob[:, :], PT[0 + pi][:, :],
                          start=True, stop=False, ctr="tt")
                    mk.op("tensor", "matmul", Z[:, :], zerob[:, :], PT[0 + pi][:, :],
                          start=True, stop=False, ctr="tt")
                mk.op("tensor", "matmul", DEN[0:1, :], onesb[:, 0:1], PT[0 + pi][:, :],
                      start=False, stop=False, tile_position=(0, 0),
                      skip_group_check=True, ctr="tt")
                mk.op("tensor", "matmul", DEN[64:65, :], onesb[:, 0:1], PT[2 + pi][:, :],
                      start=False, stop=False, tile_position=(0, 64),
                      skip_group_check=True, ctr="tt")
                vsl0 = VTM[:, ki * 256 + (hp % 2) * 128: ki * 256 + (hp % 2) * 128 + 64]
                vsl1 = VTM[:, ki * 256 + (hp % 2) * 128 + 64: ki * 256 + (hp % 2) * 128 + 128]
                mk.op("tensor", "matmul", Z[0:64, :], vsl0, PT[0 + pi][:, :],
                      start=False, stop=False, tile_position=(0, 0),
                      skip_group_check=True, ctr="tt")
                mk.op("tensor", "matmul", Z[64:128, :], vsl1, PT[2 + pi][:, :],
                      start=False, stop=False, tile_position=(0, 64),
                      skip_group_check=True, ctr="tt")
                if ki == NKI - 1:
                    mk.op("tensor", "matmul", DEN[:, :], zerob[:, :], PT[0 + pi][:, :],
                          start=False, stop=True, ctr="tt")
                    mk.op("tensor", "matmul", Z[:, :], zerob[:, :], PT[0 + pi][:, :],
                          start=False, stop=True, ctr="tt")
                Ta[v] = mk.cnt["tt"]
                if ki == NKI - 1:
                    emit_qi_tail_a(qi_g, qpar, v)

            def load_slab(hp):
                if hp >= 2:
                    mk.wait("gpsimd", "tt", proj_done_t[hp - 2])
                elif "p0" in phases:
                    mk.wait("gpsimd", "ve", mk.v_xn_done)
                slab_g[hp] = mk.dma("gpsimd",
                    WSL[hp % 2][:, :].rearrange("p (m c f) -> p m c f", m=3, c=CC),
                    din["qkvw0"].ap()[hp].rearrange("m c p f -> p m c f"))

            def load_vslab(pair):
                if pair >= 1:
                    mk.wait("gpsimd", "tt", vtm_mm_done[pair - 1])
                elif "p0" in phases:
                    mk.wait("gpsimd", "ve", mk.v_xn_done)
                for h2 in range(2):
                    mk.dma("gpsimd",
                        VSL[:, :].rearrange("p (c g f) -> p c g f", c=CC, g=2)[:, :, h2, :],
                        din["qkvw0"].ap()[2 * pair + h2, 2].rearrange("c p f -> p c f"))
                vslab_g[pair] = mk.cnt["gw"]

            load_slab(0)
            load_vslab(0)

            for hp in range(NHP):
                hpb = hp % 2
                pair = hp // 2
                if hp + 1 < NHP:
                    load_slab(hp + 1)

                mk.wait("tensor", "gw", slab_g[hp])
                if hp >= 1:
                    mk.wait("tensor", "sc", E[st1["u"] - 1])
                    mk.wait("tensor", "ve", V[st1["u"] - 1])
                else:
                    if "p0" in phases:
                        mk.wait("tensor", "ve", mk.v_xn_done)
                        mk.wait("tensor", "sc", mk.s_ln_done)
                        mk.wait("tensor", "ve", mk.cs_done_ve)
                ev_q = {}; ev_k = {}
                for qt in range(4):
                    pb = PSB[qt % 2]
                    if qt >= 2:
                        mk.wait("tensor", "ve", ev_q[qt - 2])
                    for cc in range(CC):
                        mk.op("tensor", "matmul", pb[:, :],
                              WSL[hpb][:, cc * 128:(cc + 1) * 128],
                              xn_sl(cc, qt * 512, 512),
                              start=(cc == 0), stop=(cc == CC - 1), ctr="tt")
                    mk.wait("vector", "tt", mk.cnt["tt"])
                    mk.op("vector", "tensor_copy", QB[:, qt * 512:(qt + 1) * 512],
                          pb[:, :], ctr="ve")
                    ev_q[qt] = mk.cnt["ve"]
                for nt in range(8):
                    pb = PSB[nt % 2]
                    mk.wait("tensor", "ve", ev_q[2 + nt % 2] if nt < 2 else ev_k[nt - 2])
                    for cc in range(CC):
                        mk.op("tensor", "matmul", pb[:, :],
                              WSL[hpb][:, 1024 + cc * 128:1024 + (cc + 1) * 128],
                              xn_sl(cc, nt * 512, 512),
                              start=(cc == 0), stop=(cc == CC - 1), ctr="tt")
                    mk.wait("vector", "tt", mk.cnt["tt"])
                    mk.op("vector", "tensor_copy", KB[:, nt * 512:(nt + 1) * 512],
                          pb[:, :], ctr="ve")
                    ev_k[nt] = mk.cnt["ve"]
                proj_done_t[hp] = mk.cnt["tt"]
                if hp % 2 == 0:
                    mk.wait("tensor", "gw", vslab_g[pair])
                    ev_v = {}
                    for ti in range(32):
                        pb = PSB[2 + ti % 2]
                        if ti >= 2:
                            mk.wait("tensor", "ve", ev_v[ti - 2])
                        for cc in range(CC):
                            mk.op("tensor", "matmul", pb[:, 0:256],
                                  xn_sl(cc, ti * 128, 128),
                                  VSL[:, cc * 256:(cc + 1) * 256],
                                  start=(cc == 0), stop=(cc == CC - 1), ctr="tt")
                        mk.wait("vector", "tt", mk.cnt["tt"])
                        mk.op("vector", "tensor_copy", VTM[:, ti * 256:(ti + 1) * 256],
                              pb[:, 0:256], ctr="ve")
                        ev_v[ti] = mk.cnt["ve"]
                    vtm_mm_done[pair] = mk.cnt["tt"]
                    proj_done_t[hp] = mk.cnt["tt"]
                    vtm_evac[pair] = mk.cnt["ve"]
                    if pair + 1 < 4:
                        load_vslab(pair + 1)

                mk.wait("gpsimd", "ve", ev_q[3])
                for g in range(4):
                    mk.dma("gpsimd", SW[g * 32:(g + 1) * 32, 0:TOWN],
                           QB[(g ^ 1) * 32:((g ^ 1) + 1) * 32, :])
                g_swq = mk.cnt["gw"]
                mk.wait("vector", "gw", g_swq)
                mk.op("vector", "tensor_tensor", QB[:, :], QB[:, :],
                      csc[:, 0:TOWN], ALU.mult, ctr="ve")
                mk.op("vector", "tensor_tensor", SW[:, 0:TOWN], SW[:, 0:TOWN],
                      css[:, 0:TOWN], ALU.mult, ctr="ve")
                mk.op("vector", "tensor_add", QB[:, :], QB[:, :],
                      SW[:, 0:TOWN], ctr="ve")
                v_ropeq = mk.cnt["ve"]
                mk.wait("gpsimd", "ve", max(v_ropeq, ev_k[7]))
                for g in range(4):
                    mk.dma("gpsimd", SW[g * 32:(g + 1) * 32, 0:TFULL],
                           KB[(g ^ 1) * 32:((g ^ 1) + 1) * 32, :])
                g_swk = mk.cnt["gw"]
                mk.wait("vector", "gw", g_swk)
                mk.op("vector", "tensor_tensor", KB[:, :], KB[:, :],
                      csc[:, :], ALU.mult, ctr="ve")
                mk.op("vector", "tensor_tensor", SW[:, :], SW[:, :], css[:, :],
                      ALU.mult, ctr="ve")
                mk.op("vector", "tensor_add", KB[:, :], KB[:, :],
                      SW[:, :], ctr="ve")
                rope_done[hp] = mk.cnt["ve"]

                for half in range(2):
                    for h2 in range(2):
                        head = hp * 2 + h2
                        for wh in range(2):
                            if (hp, half, h2, wh) != (0, 0, 0, 0):
                                mk.wait("sync", "sc", mk.cnt["sc"])
                            elif "p0" in phases:
                                mk.wait("sync", "sc", mk.cs_done_sc)
                                mk.wait("sync", "ve", mk.cs_done_ve)
                            hdl = mk.dma("sync", DT[:, :],
                                din["dwin"].ap()[half][:, wh * 3840:(wh + 1) * 3840])
                            mk.wait("scalar", "hw", hdl)
                            if st1["u"] >= 1:
                                mk.wait("scalar", "ve", V[st1["u"] - 1])
                            if (hp, half, h2, wh) == (0, 0, 0, 0):
                                mk.wait("scalar", "ve", mk.v_pa)
                                if "p0" in phases:
                                    mk.wait("scalar", "ve", mk.cs_done_ve)
                            mk.op("scalar", "activation", DT[:, :], DT[:, :], AF.Ln,
                                  bias=1.0, scale=acol[:, head:head + 1], ctr="sc")
                            mk.op("scalar", "activation",
                                  MW[h2][:, wh * 3840:(wh + 1) * 3840], DT[:, :],
                                  AF.Exp, bias=0.0, scale=pcoln[:, head:head + 1], ctr="sc")
                    mwdone[(hp, half)] = mk.cnt["sc"]

                    for qih in range(2):
                        qt = half * 2 + qih
                        qi_g = hp * 4 + qt
                        qpar = qi_g % 2
                        for ki in range(NKI):
                            u = st1["u"]
                            pi = u % 2
                            for it in list(st1["pend"]):
                                if u >= it[0]:
                                    emit_qi_tail_b(it[1], it[2])
                                    st1["pend"].remove(it)
                            if u >= 2:
                                mk.wait("tensor", "sc", E[u - 2])
                            if ki == 0 and qt == 0:
                                mk.wait("tensor", "ve", rope_done[hp])
                            s1 = PSB[0 + pi]; s2 = PSB[2 + pi]
                            mk.op("tensor", "matmul", s1[:, :],
                                  KB[0:64, ki * 128:(ki + 1) * 128],
                                  QB[0:64, qt * 512:(qt + 1) * 512],
                                  start=True, stop=True, tile_position=(0, 0), ctr="tt")
                            mk.op("tensor", "matmul", s2[:, :],
                                  KB[64:128, ki * 128:(ki + 1) * 128],
                                  QB[64:128, qt * 512:(qt + 1) * 512],
                                  start=True, stop=True, tile_position=(64, 0), ctr="tt")
                            Ts[u] = mk.cnt["tt"]
                            if u >= 1:
                                emit_denav(u - 1)
                            mk.wait("scalar", "tt", Ts[u])
                            if u >= 2:
                                mk.wait("scalar", "ve", V[u - 2])
                            mk.op("scalar", "activation", EXT[0 + pi][:, :], s1[:, :],
                                  AF.Exp, bias=0.0, scale=0.125, ctr="sc")
                            mk.op("scalar", "activation", EXT[2 + pi][:, :], s2[:, :],
                                  AF.Exp, bias=0.0, scale=0.125, ctr="sc")
                            E[u] = mk.cnt["sc"]
                            mk.wait("vector", "sc", max(E[u], mwdone[(hp, half)]))
                            if u >= 2:
                                mk.wait("vector", "tt", Ta[u - 2])
                            mo = moff(qih, ki)
                            mk.op("vector", "tensor_tensor", PT[0 + pi][:, :],
                                  EXT[0 + pi][:, :], MW[0][:, mo:mo + 512], ALU.mult, ctr="ve")
                            mk.op("vector", "tensor_tensor", PT[2 + pi][:, :],
                                  EXT[2 + pi][:, :], MW[1][:, mo:mo + 512], ALU.mult, ctr="ve")
                            V[u] = mk.cnt["ve"]
                            st1["meta"][u] = (hp, qt, qi_g, qpar, ki, pi)
                            st1["u"] += 1
            emit_denav(st1["u"] - 1)
            for it in list(st1["pend"]):
                emit_qi_tail_b(it[1], it[2])
                st1["pend"].remove(it)
            mk.p1 = st1

        def phase2():
            zall = reg(0, NHP * TOWN, BF16)
            ows = reg(52, CC * CC * 128, BF16)
            sst = [reg(32, 512), reg(34, 512)]
            for hp in range(NHP):
                mk.wait("gpsimd", "hw", mk.p1["zspill"][hp])
            mk.dma("gpsimd", ows[:, :].rearrange("p (z d f) -> p z d f", z=CC, d=CC),
                   din["outw0"].ap().rearrange("z d p f -> p z d f"))
            g_ow = mk.cnt["gw"]
            mk.wait("gpsimd", "ve", mk.p1["V"][mk.p1["u"] - 1])
            g_z = mk.dma("gpsimd", zall[:, :].rearrange("p (h t) -> p h t", h=NHP),
                         d_z0.ap().rearrange("h p t -> p h t"))
            ev_s = {}
            for dc in range(CC):
                for t4 in range(4):
                    j = dc * 4 + t4
                    pb = PSB[j % 2]
                    if j == 0:
                        mk.wait("tensor", "gw", g_ow)
                        mk.wait("tensor", "gw", g_z)
                        mk.wait("tensor", "sc", mk.cnt["sc"])
                        mk.wait("tensor", "ve", mk.cnt["ve"])
                    if j >= 2:
                        mk.wait("tensor", "sc", ev_s[j - 2])
                    for zc in range(CC):
                        mk.op("tensor", "matmul", pb[:, :],
                              ows[:, (zc * CC + dc) * 128:(zc * CC + dc + 1) * 128],
                              zall[:, zc * TOWN + t4 * 512: zc * TOWN + (t4 + 1) * 512],
                              start=(zc == 0), stop=(zc == CC - 1), ctr="tt")
                    mk.wait("scalar", "tt", mk.cnt["tt"])
                    if j >= 2:
                        mk.wait("scalar", "hw", sdma[j - 2])
                    mk.op("scalar", "copy", sst[j % 2][:, :], pb[:, :], ctr="sc")
                    ev_s[j] = mk.cnt["sc"]
                    mk.wait("sync", "sc", ev_s[j])
                    sdma[j] = mk.dma("sync", d_s0.ap()[dc][:, t4 * 512:(t4 + 1) * 512],
                                     sst[j % 2][:, :])
            mk.p2_sdma = dict(sdma)

        sdma = {}
        if "p1" in phases:
            phase1()
        if "p2" in phases:
            phase2()
        def xns_reg(off_kb, nelem, dt=BF16):
            off = int(off_kb * 1024)
            nbytes = nelem * (4 if dt == F32 else 2)
            assert off + nbytes <= CC * TOWN * 2
            a = XNs[:, off // 2:(off + nbytes) // 2]
            if dt != BF16:
                a = a.bitcast(dt)
            return a

        def h_sl(ft, t0, n):
            if ft < 30:
                return reg(4 * ft, TOWN, BF16)[:, t0:t0 + n]
            return xns_reg(4 * (ft - 30), TOWN)[:, t0:t0 + n]

        FB1 = 2
        NSL1 = 16
        R1 = 3
        fc1ring = [xns_reg(8 + 4 * i, CC * FB1 * 128) for i in range(R1)]
        R2 = 3
        fc2ring = [xns_reg(20 + 2 * i, 8 * 128) for i in range(R2)]
        mask_bc = xns_reg(26, TOWN)
        lgts = recs[0:8, :]
        lgtm = rbcb[:, 0:32]
        mxcol = rbcb[:, 32:36]
        mtm = recb[:, 0:16].bitcast(BF16)
        yst = [recs, rbcb]

        def phase3():
            if dbg:
                t_lgtm = dbg_out("t_lgtm", (4, 128, 32))
                t_mx = dbg_out("t_mx", (4, 128, 4))
                t_mtm = dbg_out("t_mtm", (4, 128, 32), BF16)
            maskem = masksL1[0:8, 0:TOWN]
            if "p1" in phases:
                mk.wait("sync", "ve", mk.cnt["ve"])
            mk.dma("sync", rowsel[:, :], d_rowsel.ap())
            mk.dma("sync", recb[0:8, 16:24], d_idf8.ap())
            h_c3 = mk.cnt["hw"]
            mk.wait("tensor", "hw", h_c3)
            base_t = mk.cnt["tt"]; base_s = mk.cnt["sc"]; base_v = mk.cnt["ve"]
            mk.wait("tensor", "sc", base_s)
            mk.wait("tensor", "ve", base_v)
            if "p0" in phases:
                mk.wait("tensor", "gg", mk.cnt["gg"])
                mk.wait("vector", "hw", mk.h_lg_stored)
            for tt in range(4):
                if tt > 0:
                    mk.wait("sync", "tt", t_tr)
                hl = mk.dma("sync", lgts[:, :], d_lg.ap()[:, tt * 512:(tt + 1) * 512])
                mk.wait("tensor", "hw", hl)
                if tt > 0:
                    mk.wait("tensor", "ve", v_mask)
                    mk.wait("tensor", "sc", s_me)
                for j in range(4):
                    mk.op("tensor", "transpose", PSB[2][:, j * 8:(j + 1) * 8],
                          lgts[:, j * 128:(j + 1) * 128], recb[0:8, 16:24], ctr="tt")
                t_tr = mk.cnt["tt"]
                mk.wait("scalar", "tt", t_tr)
                mk.op("scalar", "copy", lgtm[:, :], PSB[2][:, 0:32], ctr="sc")
                s_tr = mk.cnt["sc"]
                mk.wait("vector", "sc", s_tr)
                for j in range(4):
                    mk.op("vector", "reduce_max", mxcol[:, j:j + 1],
                          lgtm[:, j * 8:(j + 1) * 8], axis=AX.X, ctr="ve")
                for j in range(4):
                    mk.op("vector", "tensor_single_scalar", mtm[:, j * 8:(j + 1) * 8],
                          lgtm[:, j * 8:(j + 1) * 8], mxcol[:, j:j + 1], ALU.is_ge, ctr="ve")
                v_mask = mk.cnt["ve"]
                if dbg:
                    mk.wait("sync", "sc", s_tr)
                    mk.dma("sync", t_lgtm.ap()[tt], lgtm[:, :])
                    mk.wait("sync", "ve", v_mask)
                    mk.dma("sync", t_mx.ap()[tt], mxcol[:, :])
                    mk.dma("sync", t_mtm.ap()[tt], mtm[:, :])
                    h_tap3 = mk.cnt["hw"]
                    mk.wait("scalar", "hw", h_tap3)
                    mk.wait("vector", "hw", h_tap3)
                mk.wait("tensor", "ve", v_mask)
                for j in range(4):
                    mk.op("tensor", "transpose",
                          PSB[2][0:8, j * 64:(j + 1) * 64].bitcast(BF16),
                          mtm[:, j * 8:(j + 1) * 8], identb[:, :], ctr="tt")
                t_tb = mk.cnt["tt"]
                mk.wait("scalar", "tt", t_tb)
                mk.op("scalar", "copy", maskem[:, tt * 512:(tt + 1) * 512],
                      PSB[2][0:8, 0:256].bitcast(BF16), ctr="sc")
                s_me = mk.cnt["sc"]
            mask_done_s = mk.cnt["sc"]
            if dbg:
                t_mask3 = dbg_out("t_mask3", (8, TOWN), BF16)
                mk.wait("sync", "sc", mask_done_s)
                mk.dma("sync", t_mask3.ap(), maskem[:, :])

            slot1_g = {}; slot2_g = {}
            slot1_lastmm = {}; slot2_lastmm = {}
            load_q = []

            def load_slot1(e, k):
                gk = e * NSL1 + k
                if gk >= R1:
                    mk.wait("gpsimd", "tt", slot1_lastmm[gk - R1])
                slot1_g[gk] = mk.dma("gpsimd", fc1ring[gk % R1][:, :],
                                     din["w1p"].ap()[e, k].rearrange("p c f -> p (c f)"))

            def load_slot2(e, k):
                gk = e * 32 + k
                if gk >= R2:
                    mk.wait("gpsimd", "tt", slot2_lastmm[gk - R2])
                dc, fcb = divmod(k, 4)
                slot2_g[gk] = mk.dma("gpsimd",
                                     fc2ring[gk % R2][:, :].rearrange("p (f d) -> p f d", f=8),
                                     din["w2p"].ap()[e, dc, fcb * 8:(fcb + 1) * 8]
                                     .rearrange("f p d -> p f d"))

            fc2_done_t = {}
            h_done_s = {}
            for e in range(NE):
                mk.wait("tensor", "sc", mask_done_s)
                if e > 0:
                    mk.wait("tensor", "sc", mbc_ev)
                    mk.wait("tensor", "ve", mk.cnt["ve"])
                for tt in range(4):
                    if tt > 0:
                        mk.wait("tensor", "sc", mk.cnt["sc"])
                    mk.op("tensor", "matmul", PSB[3][:, :],
                          rowsel[:, e * 128:(e + 1) * 128],
                          maskem[:, tt * 512:(tt + 1) * 512],
                          start=True, stop=True, ctr="tt")
                    mk.wait("scalar", "tt", mk.cnt["tt"])
                    mk.op("scalar", "copy", mask_bc[:, tt * 512:(tt + 1) * 512],
                          PSB[3][:, :], ctr="sc")
                mbc_ev = mk.cnt["sc"]

                load_slot1(e, 0)
                load_slot1(e, 1)
                for ft in range(FT):
                    k = ft // FB1
                    gk = e * NSL1 + k
                    if ft % FB1 == 0 and k + 2 < NSL1:
                        load_slot1(e, k + 2)
                    fti = ft % FB1
                    for tt in range(4):
                        j = ft * 4 + tt
                        pb = PSB[j % 2]
                        if ft == 0 and tt == 0:
                            if e == 0:
                                mk.wait("tensor", "sc", mask_done_s)
                                mk.wait("tensor", "ve", mk.cnt["ve"])
                            else:
                                mk.wait("tensor", "tt", fc2_done_t[e - 1])
                        if j >= 2:
                            mk.wait("tensor", "sc", relu_ev[j - 2])
                        mk.wait("tensor", "gw", slot1_g[gk])
                        for cc in range(CC):
                            mk.op("tensor", "matmul", pb[:, :],
                                  fc1ring[gk % R1][:, cc * FB1 * 128 + fti * 128:
                                                   cc * FB1 * 128 + (fti + 1) * 128],
                                  XNo[:, cc * TOWN + tt * 512: cc * TOWN + (tt + 1) * 512],
                                  start=(cc == 0), stop=(cc == CC - 1), ctr="tt")
                        if tt == 3 and fti == FB1 - 1:
                            slot1_lastmm[gk] = mk.cnt["tt"]
                        mk.wait("scalar", "tt", mk.cnt["tt"])
                        if e > 0 and ft == 0 and tt == 0:
                            mk.wait("scalar", "tt", fc2_done_t[e - 1])
                        mk.op("scalar", "activation", h_sl(ft, tt * 512, 512),
                              pb[:, :], AF.Relu,
                              bias=b1ps[:, e * FT + ft:e * FT + ft + 1], scale=1.0, ctr="sc")
                        relu_ev[j] = mk.cnt["sc"]
                h_done_s[e] = mk.cnt["sc"]

                load_slot2(e, 0)
                load_slot2(e, 1)
                for dc in range(CC):
                    for fc in range(FT):
                        k2 = dc * 4 + fc // 8
                        gk2 = e * 32 + k2
                        if fc % 8 == 0 and k2 + 2 < 32:
                            load_slot2(e, k2 + 2)
                        if fc % 8 == 0:
                            mk.wait("tensor", "gw", slot2_g[gk2])
                        if fc == 0:
                            mk.wait("tensor", "sc", h_done_s[e])
                            if dc == 0 and e == 0:
                                mk.wait("tensor", "ve", mk.cnt["ve"])
                            elif dc > 0:
                                mk.wait("tensor", "ve", y_ev[(e, dc - 1)])
                            else:
                                mk.wait("tensor", "ve", y_ev[(e - 1, CC - 1)])
                        for tt in range(4):
                            mk.op("tensor", "matmul", PSB[4 + tt][:, :],
                                  fc2ring[gk2 % R2][:, (fc % 8) * 128:(fc % 8 + 1) * 128],
                                  h_sl(fc, tt * 512, 512),
                                  start=(fc == 0), stop=(fc == FT - 1), ctr="tt")
                        if fc % 8 == 7:
                            slot2_lastmm[gk2] = mk.cnt["tt"]
                    t_fc2 = mk.cnt["tt"]
                    for tt in range(4):
                        mk.wait("vector", "tt", t_fc2)
                        mk.wait("vector", "sc", mbc_ev)
                        if (e, dc) != (0, 0) or tt >= 2:
                            prev = ydma.get((e, dc, tt - 2), ydma.get((e, dc - 1, tt + 2),
                                    ydma.get((e - 1, CC - 1, tt + 2), None)))
                            if prev is not None:
                                mk.wait("vector", "gw", prev)
                        mk.op("vector", "scalar_tensor_tensor", yst[tt % 2][:, :],
                              PSB[4 + tt][:, :], b2ps[:, e * CC + dc:e * CC + dc + 1],
                              mask_bc[:, tt * 512:(tt + 1) * 512], ALU.add, ALU.mult, ctr="ve")
                        y_ev[(e, dc)] = mk.cnt["ve"]
                        mk.wait("gpsimd", "ve", y_ev[(e, dc)])
                        if e == 0:
                            ydma[(e, dc, tt)] = mk.dma("gpsimd",
                                d_y.ap()[dc][:, tt * 512:(tt + 1) * 512], yst[tt % 2][:, :])
                        else:
                            ydma[(e, dc, tt)] = mk.dma("gpsimd",
                                d_y.ap()[dc][:, tt * 512:(tt + 1) * 512], yst[tt % 2][:, :],
                                accum_op=ALU.add)
                    fc2_done_t[e] = mk.cnt["tt"]
            mk.p3_ydma = mk.cnt["gw"]
            mk.p3_done = dict(tt=mk.cnt["tt"], sc=mk.cnt["sc"], ve=mk.cnt["ve"])

        relu_ev = {}; y_ev = {}; ydma = {}
        if "p3" in phases:
            phase3()

        def phase4():
            xst = reg(0, CC * 512)
            sst4 = reg(16, CC * 512)
            yst4 = reg(32, CC * 512)
            gst = reg(48, CC * 512)
            x1st = reg(64, CC * 512)
            mub4 = reg(80, 512); rb4 = reg(82, 512); murb4 = reg(84, 512)
            scr14 = reg(86, 512); scr24 = reg(88, 512); st24 = reg(90, 1024)
            x1b = reg(94, CC * 512, BF16)
            x2st4 = [reg(110, 512, BF16), reg(111, 512, BF16)]
            SUMB, SQB = PSB[0], PSB[1]
            s_statdone = mk.cnt["sc"]
            base_ve = mk.cnt["ve"]
            ccdma = 0
            x1dma = 0
            for tt in range(4):
                if tt >= 1:
                    mk.wait("sync", "ve", x1_done[tt - 1])
                    if x1dma:
                        mk.wait("sync", "hw", x1dma)
                if tt == 0 and "p3" in phases:
                    mk.wait("sync", "gw", mk.p3_ydma)
                    mk.wait("sync", "tt", mk.p3_done["tt"])
                hx = mk.dma("sync", xst[:, :].rearrange("p (c t) -> p c t", c=CC),
                            din["xt"].ap().rearrange("(c p) t -> p c t", p=128)
                            [:, :, tt * 512:(tt + 1) * 512])
                hs = mk.dma("sync", sst4[:, :].rearrange("p (c t) -> p c t", c=CC),
                            d_s0.ap().rearrange("c p t -> p c t")[:, :, tt * 512:(tt + 1) * 512])
                hy = mk.dma("sync", yst4[:, :].rearrange("p (c t) -> p c t", c=CC),
                            d_y.ap().rearrange("c p t -> p c t")[:, :, tt * 512:(tt + 1) * 512])
                h_ld = mk.cnt["hw"]
                mk.wait("scalar", "hw", h_ld)
                if tt >= 1:
                    mk.wait("scalar", "ve", x1_done[tt - 1])
                for cc in range(CC):
                    mk.op("scalar", "activation", gst[:, cc * 512:(cc + 1) * 512],
                          yst4[:, cc * 512:(cc + 1) * 512], AF.Silu, ctr="sc")
                s_g = mk.cnt["sc"]
                mk.wait("vector", "sc", s_g)
                if tt == 0 and "p3" in phases:
                    mk.wait("vector", "sc", mk.p3_done["sc"])
                ev4 = {}
                for cc in range(CC):
                    k = tt * CC + cc
                    sl = slice(cc * 512, (cc + 1) * 512)
                    mk.op("vector", "tensor_tensor", gst[:, sl], sst4[:, sl],
                          gst[:, sl], ALU.mult, ctr="ve")
                    mk.op("vector", "tensor_add", x1st[:, sl], xst[:, sl], gst[:, sl], ctr="ve")
                    mk.op("vector", "tensor_copy", x1b[:, sl].bitcast(BF16)
                          if False else x1b[:, sl], x1st[:, sl], ctr="ve")
                    if k >= 2:
                        mk.wait("vector", "tt", mm4_sq[k - 2])
                    mk.op("vector", "tensor_tensor", x2st4[k % 2][:, :],
                          x1b[:, sl], x1b[:, sl], ALU.mult, ctr="ve")
                    ev4[cc] = mk.cnt["ve"]
                    if cc == 0:
                        mk.wait("tensor", "sc", s_statdone)
                        if tt == 0 and "p3" in phases:
                            mk.wait("tensor", "ve", mk.p3_done["ve"])
                    mk.wait("tensor", "ve", ev4[cc])
                    mk.op("tensor", "matmul", SUMB[0:1, :], onesb[:, 0:1], x1b[:, sl],
                          start=(cc == 0), stop=(cc == CC - 1), ctr="tt")
                    mk.op("tensor", "matmul", SQB[0:1, :], onesb[:, 0:1], x2st4[k % 2][:, :],
                          start=(cc == 0), stop=(cc == CC - 1), ctr="tt")
                    mm4_sq[k] = mk.cnt["tt"]
                mk.wait("sync", "ve", mk.cnt["ve"])
                x1dma = mk.dma("sync", d_x1.ap().rearrange("c p t -> p c t")
                               [:, :, tt * 512:(tt + 1) * 512],
                               x1st[:, :].rearrange("p (c t) -> p c t", c=CC))
                t_stats = mk.cnt["tt"]
                mk.wait("scalar", "tt", t_stats)
                mk.op("scalar", "copy", st24[0:1, 0:512], SUMB[0:1, :], ctr="sc")
                mk.op("scalar", "copy", st24[0:1, 512:1024], SQB[0:1, :], ctr="sc")
                s_ev = mk.cnt["sc"]
                mk.wait("tensor", "sc", s_ev)
                mk.op("tensor", "matmul", SUMB[:, :], ones1f[:, :], st24[0:1, 0:512],
                      start=True, stop=True, ctr="tt")
                mk.op("tensor", "matmul", SQB[:, :], ones1f[:, :], st24[0:1, 512:1024],
                      start=True, stop=True, ctr="tt")
                t_bc = mk.cnt["tt"]
                mk.wait("scalar", "tt", t_bc)
                mk.op("scalar", "mul", mub4[:, :], SUMB[:, :], 1.0 / D, ctr="sc")
                mk.op("scalar", "mul", scr14[:, :], SQB[:, :], 1.0 / D, ctr="sc")
                s_bc = mk.cnt["sc"]
                mk.wait("vector", "sc", s_bc)
                mk.op("vector", "tensor_tensor", scr24[:, :], mub4[:, :], mub4[:, :], ALU.mult, ctr="ve")
                mk.op("vector", "tensor_sub", scr14[:, :], scr14[:, :], scr24[:, :], ctr="ve")
                mk.op("vector", "tensor_single_scalar", scr14[:, :], scr14[:, :], EPS, ALU.add, ctr="ve")
                mk.op("vector", "reciprocal", scr24[:, :], scr14[:, :], ctr="ve")
                v_rc = mk.cnt["ve"]
                mk.wait("scalar", "ve", v_rc)
                mk.op("scalar", "activation", rb4[:, :], scr24[:, :], AF.Sqrt, ctr="sc")
                s_statdone = mk.cnt["sc"]
                mk.wait("vector", "sc", s_statdone)
                mk.op("vector", "tensor_tensor", murb4[:, :], mub4[:, :], rb4[:, :], ALU.mult, ctr="ve")
                for cc in range(CC):
                    sl = slice(cc * 512, (cc + 1) * 512)
                    dst = XNo[:, cc * TOWN + tt * 512: cc * TOWN + (tt + 1) * 512]
                    mk.op("vector", "tensor_tensor", dst, x1b[:, sl], rb4[:, :], ALU.mult, ctr="ve")
                    mk.op("vector", "tensor_sub", dst, dst, murb4[:, :], ctr="ve")
                x1_done[tt] = mk.cnt["ve"]
                mk.wait("sync", "ve", x1_done[tt])
                ccdma = mk.dma("sync", cc_in.ap().rearrange("p (c t) -> p c t", c=CC)
                               [:, :, tt * 512:(tt + 1) * 512],
                               XNo[:, :].rearrange("p (c t) -> p c t", c=CC)
                               [:, :, tt * 512:(tt + 1) * 512])
            mk.wait("gpsimd", "hw", ccdma)
            mk.cnt["cc"] += 1
            sem_cc = mk.sem["cc"]
            mk.em("gpsimd", lambda E, s=sem_cc: E.collective_compute(
                "AllGather", ALU.bypass,
                replica_groups=[[0, 1], [2, 3], [4, 5], [6, 7]],
                ins=[cc_in.ap().opt()], outs=[cc_out.ap().opt()]).then_inc(s, 1))
            mk.p4_cc = mk.cnt["cc"]
            mk.p4_done = dict(tt=mk.cnt["tt"], sc=mk.cnt["sc"], ve=mk.cnt["ve"],
                              hw=mk.cnt["hw"])

        mm4_sq = {}; x1_done = {}
        if "p4" in phases:
            phase4()

        def arm_helpers():
            def init_role(E):
                reg_ = E.alloc_register("mk_role")
                E.load(reg_, paramsb[0:1, 0:1])
                E._mk_rolereg = reg_
            def if_begin(E):
                E._mk_ifg = E.If_eq(E._mk_rolereg, 0)
                E._mk_ifg.__enter__()
            def else_begin(E):
                E._mk_ifg.__exit__(None, None, None)
                E._mk_elg = E.Else()
                E._mk_elg.__enter__()
            def arm_end(E):
                E._mk_elg.__exit__(None, None, None)
            return init_role, if_begin, else_begin, arm_end

        ARM_ENGS = ("tensor", "scalar", "vector", "sync")

        def visible_kis(role, qi):
            qg0 = GSEG[role][qi // 2] + 512 * (qi % 2)
            out = []
            for ki in range(NKI):
                kg0 = GSEG[role][ki // 8] + 128 * (ki % 8)
                if kg0 <= qg0 + 384:
                    pat = (kg0 - qg0) // 128 if kg0 >= qg0 else None
                    out.append((ki, pat))
            return out

        def phase5():
            init_role, if_begin, else_begin, arm_end = arm_helpers()
            for ctr_ in ("tt", "sc", "ve", "gw", "hw"):
                pass
            mk.wait("vector", "tt", mk.cnt["tt"])
            mk.op("vector", "memset", masksL1[:, :], 0.0, ctr="ve")
            v_m0 = mk.cnt["ve"]
            mk.wait("gpsimd", "ve", v_m0)
            for pi_, off in enumerate((0, 128, 256, 384)):
                mk.op("gpsimd", "affine_select", masksL1[:, pi_ * 512:(pi_ + 1) * 512],
                      masksL1[:, pi_ * 512:(pi_ + 1) * 512], [[1, 512]], ALU.is_ge,
                      -1e6, base=-off, channel_multiplier=-1, ctr="gg")
            mk.g_masks5 = mk.cnt["gg"]

            if ARMS:
                for eng in ARM_ENGS:
                    mk.em(eng, init_role)
                    mk.wait(eng, "hw", mk.h_small)

            mk.wait("sync", "cc", mk.p4_cc)
            mk.wait("sync", "ve", mk.p4_done["ve"])
            if ARMS:
                snap = dict(mk.cnt)
                mk.em("sync", if_begin)
                mk.dma("sync", XNs[:, :], cc_out.ap()[1].rearrange("p f -> p f"))
                mk.cnt.update({k: snap[k] for k in mk.cnt})
                mk.cnt["hw"] = snap["hw"]
                mk.em("sync", else_begin)
                mk.dma("sync", XNs[:, :], cc_out.ap()[0].rearrange("p f -> p f"))
                mk.em("sync", arm_end)
            else:
                mk.dma("sync", XNs[:, :], cc_out.ap()[1].rearrange("p f -> p f"))
            h_sib = mk.cnt["hw"]

            dq_q = reg(0, 4096)
            dq_s = reg(16, 4096)
            dq_o = reg(32, 4096, BF16)
            mk.wait("gpsimd", "ve", mk.p4_done["ve"])
            dq_v = mk.cnt["ve"]
            dq_dma = 0
            for ch in (range(CC + FT) if DEQ else []):
                fc1 = ch < CC
                n = DFF if fc1 else D
                if dq_dma:
                    mk.wait("gpsimd", "hw", dq_dma)
                g1 = mk.dma("gpsimd", dq_q[:, 0:n],
                            (din["fc1q"].ap().rearrange("(a p) f -> a p f", p=128)[ch]
                             if fc1 else
                             din["fc2q"].ap().rearrange("(a p) f -> a p f", p=128)[ch - CC]))
                g2 = mk.dma("gpsimd", dq_s[:, 0:n],
                            (din["fc1s"].ap()[ch:ch + 1, :] if fc1 else
                             din["fc2s"].ap()[ch - CC:ch - CC + 1, :]).partition_broadcast(128))
                mk.wait("vector", "gw", g2)
                mk.op("vector", "scalar_tensor_tensor", dq_o[:, 0:n], dq_q[:, 0:n],
                      -128.0, dq_s[:, 0:n], ALU.add, ALU.mult, ctr="ve")
                dq_v = mk.cnt["ve"]
                mk.wait("sync", "ve", dq_v)
                if fc1:
                    dq_dma = mk.dma("sync", d_w1bf.ap()[ch], dq_o[:, 0:n])
                    if ch == CC - 1:
                        mk.dq_w1_done = dq_dma
                else:
                    fc_ = ch - CC
                    dq_dma = mk.dma("sync",
                                    d_w2bf.ap().rearrange("dc p f -> p dc f")
                                    [:, :, fc_ * 128:(fc_ + 1) * 128],
                                    dq_o[:, 0:n].rearrange("p (dc g) -> p dc g", dc=CC))
            mk.dq_done = dq_dma
            if not DEQ:
                mk.dq_w1_done = 0

            E5 = {}; Ta5 = {}
            proj_done_t = {}; slab_g = {}; vslab_g = {}; vtm_mm_done = {}
            vtm_evac = {}
            st5 = dict(u=0, zspill={}, rbc_evac={}, z_evac={}, recip_v={},
                       denev_s={}, bc_t={})

            def load_slab5(hp):
                if hp >= 2:
                    mk.wait("gpsimd", "tt", proj_done_t[hp - 2])
                else:
                    mk.wait("gpsimd", "ve", mk.p4_done["ve"])
                slab_g[hp] = mk.dma("gpsimd",
                    WSL[hp % 2][:, :].rearrange("p (m c f) -> p m c f", m=3, c=CC),
                    din["qkvw1"].ap()[hp].rearrange("m c p f -> p m c f"))

            def load_vslab5(pair):
                if pair >= 1:
                    mk.wait("gpsimd", "tt", vtm_mm_done[pair - 1])
                else:
                    mk.wait("gpsimd", "ve", mk.p4_done["ve"])
                for h2 in range(2):
                    mk.dma("gpsimd",
                        VSL[:, :].rearrange("p (c g f) -> p c g f", c=CC, g=2)[:, :, h2, :],
                        din["qkvw1"].ap()[2 * pair + h2, 2].rearrange("c p f -> p c f"))
                vslab_g[pair] = mk.cnt["gw"]

            def emit_tail5_a(zq, qpar, taval):
                if zq >= 1:
                    mk.wait("scalar", "ve", st5["recip_v"][zq - 1])
                mk.wait("scalar", "tt", taval)
                DEN = PSB[6 + qpar]
                mk.op("scalar", "copy", recs[0:1, :], DEN[0:1, :], ctr="sc")
                mk.op("scalar", "copy", recs[64:65, :], DEN[64:65, :], ctr="sc")
                st5["denev_s"][zq] = mk.cnt["sc"]
                if zq >= 1:
                    mk.wait("vector", "tt", st5["bc_t"][zq - 1])
                mk.wait("vector", "sc", st5["denev_s"][zq])
                mk.op("vector", "reciprocal", recb[0:1, :], recs[0:1, :], ctr="ve")
                mk.op("vector", "reciprocal", recb[64:65, :], recs[64:65, :], ctr="ve")
                st5["recip_v"][zq] = mk.cnt["ve"]

            def emit_tail5_b(zq, qpar, hp, qt):
                DEN = PSB[6 + qpar]
                Z = PSB[4 + qpar]
                mk.wait("tensor", "ve", st5["recip_v"][zq])
                mk.op("tensor", "matmul", DEN[:, :], hs128[:, :], recb[:, :],
                      start=True, stop=True, ctr="tt")
                st5["bc_t"][zq] = mk.cnt["tt"]
                if zq >= 1:
                    mk.wait("scalar", "ve", st5["z_evac"][zq - 1])
                mk.wait("scalar", "tt", st5["bc_t"][zq])
                mk.op("scalar", "copy", rbcb[:, :], DEN[:, :], ctr="sc")
                st5["rbc_evac"][zq] = mk.cnt["sc"]
                mk.wait("vector", "sc", st5["rbc_evac"][zq])
                if qt == 0 and hp >= 1:
                    mk.wait("vector", "hw", st5["zspill"][hp - 1])
                elif qt == 0 and "p1" in phases:
                    mk.wait("vector", "hw", mk.p1["zspill"][NHP - 1])
                mk.op("vector", "tensor_tensor", ZSB[:, qt * 512:(qt + 1) * 512],
                      Z[:, :], rbcb[:, :], ALU.mult, ctr="ve")
                st5["z_evac"][zq] = mk.cnt["ve"]
                if qt == 3:
                    mk.wait("sync", "ve", st5["z_evac"][zq])
                    st5["zspill"][hp] = mk.dma("sync", d_z1.ap()[hp], ZSB[:, :])

            mk.p5_tailb = emit_tail5_b
            load_slab5(0)
            load_vslab5(0)
            for hp in range(NHP):
                hpb = hp % 2
                pair = hp // 2
                if hp + 1 < NHP:
                    load_slab5(hp + 1)
                mk.wait("tensor", "gw", slab_g[hp])
                if hp >= 1 and UNITS5 and (hp - 1) < N_HP5:
                    mk.wait("tensor", "sc", E5[(hp - 1, "last")])
                elif hp >= 1:
                    mk.wait("tensor", "ve", mk.cnt["ve"])
                else:
                    mk.wait("tensor", "sc", mk.cnt["sc"])
                    mk.wait("tensor", "ve", mk.cnt["ve"])
                    mk.wait("tensor", "hw", h_sib)
                ev_q = {}; ev_k = {}
                for qt in range(4):
                    pb = PSB[qt % 2]
                    if qt >= 2:
                        mk.wait("tensor", "ve", ev_q[qt - 2])
                    for cc in range(CC):
                        mk.op("tensor", "matmul", pb[:, :],
                              WSL[hpb][:, cc * 128:(cc + 1) * 128],
                              xn_sl(cc, qt * 512, 512),
                              start=(cc == 0), stop=(cc == CC - 1), ctr="tt")
                    mk.wait("vector", "tt", mk.cnt["tt"])
                    mk.op("vector", "tensor_copy", QB[:, qt * 512:(qt + 1) * 512],
                          pb[:, :], ctr="ve")
                    ev_q[qt] = mk.cnt["ve"]
                for nt in range(8):
                    pb = PSB[nt % 2]
                    mk.wait("tensor", "ve", ev_q[2 + nt % 2] if nt < 2 else ev_k[nt - 2])
                    for cc in range(CC):
                        mk.op("tensor", "matmul", pb[:, :],
                              WSL[hpb][:, 1024 + cc * 128:1024 + (cc + 1) * 128],
                              xn_sl(cc, nt * 512, 512),
                              start=(cc == 0), stop=(cc == CC - 1), ctr="tt")
                    mk.wait("vector", "tt", mk.cnt["tt"])
                    mk.op("vector", "tensor_copy", KB[:, nt * 512:(nt + 1) * 512],
                          pb[:, :], ctr="ve")
                    ev_k[nt] = mk.cnt["ve"]
                proj_done_t[hp] = mk.cnt["tt"]
                if hp % 2 == 0:
                    mk.wait("tensor", "gw", vslab_g[pair])
                    if pair >= 1 and UNITS5 and (2 * pair - 1) < N_HP5:
                        mk.wait("tensor", "tt", Ta5[(2 * pair - 1, "last")])
                    ev_v = {}
                    for ti in range(32):
                        pb = PSB[2 + ti % 2]
                        if ti >= 2:
                            mk.wait("tensor", "ve", ev_v[ti - 2])
                        for cc in range(CC):
                            mk.op("tensor", "matmul", pb[:, 0:256],
                                  xn_sl(cc, ti * 128, 128),
                                  VSL[:, cc * 256:(cc + 1) * 256],
                                  start=(cc == 0), stop=(cc == CC - 1), ctr="tt")
                        mk.wait("vector", "tt", mk.cnt["tt"])
                        mk.op("vector", "tensor_copy", VTM[:, ti * 256:(ti + 1) * 256],
                              pb[:, 0:256], ctr="ve")
                        ev_v[ti] = mk.cnt["ve"]
                    vtm_mm_done[pair] = mk.cnt["tt"]
                    proj_done_t[hp] = mk.cnt["tt"]
                    vtm_evac[pair] = mk.cnt["ve"]
                    if pair + 1 < 4:
                        load_vslab5(pair + 1)
                v_ready = mk.cnt["ve"]

                if not UNITS5 or hp >= N_HP5:
                    pass
                elif not ARMS:
                    emit_units5(hp, 0, v_ready, E5, Ta5, st5,
                                emit_tail5_a, emit_tail5_b)
                else:
                    CTR_OF = {"tensor": "tt", "scalar": "sc", "vector": "ve"}
                    def make_pad(cell, eng):
                        ctr = CTR_OF.get(eng)
                        if ctr is None:
                            return lambda E: None
                        sem = mk.sem[ctr]
                        def f(E, sem=sem, cell=cell, ctr=ctr):
                            d = cell.get(ctr, 0)
                            if d:
                                E.sem_inc(sem, d)
                        return f
                    snap = dict(mk.cnt)
                    for eng in ARM_ENGS:
                        mk.em(eng, if_begin)
                    ends = {}
                    pads = {0: {}, 1: {}}
                    for role in (0, 1):
                        if role == 1:
                            for k in mk.cnt:
                                mk.cnt[k] = snap[k]
                            for eng in ARM_ENGS:
                                mk.em(eng, else_begin)
                        emit_units5(hp, role, v_ready, E5, Ta5, st5,
                                    emit_tail5_a, emit_tail5_b)
                        ends[role] = dict(mk.cnt)
                        for eng in ARM_ENGS:
                            mk.em(eng, make_pad(pads[role], eng))
                    for eng in ARM_ENGS:
                        mk.em(eng, arm_end)
                    target = {k: max(ends[0][k], ends[1][k]) for k in mk.cnt}
                    for k in ("hw", "gw", "gg", "cc"):
                        assert ends[0][k] == ends[1][k], f"dma count mismatch {k} hp{hp}"
                    for role in (0, 1):
                        for k in ("tt", "sc", "ve"):
                            pads[role][k] = target[k] - ends[role][k]
                    mk.cnt.update(target)
            mk.p5_done = dict(tt=mk.cnt["tt"], sc=mk.cnt["sc"], ve=mk.cnt["ve"])
            mk.p5_zspill = dict(st5["zspill"])

        def emit_units5(hp, role, v_ready, E5, Ta5, st5, tail_a, tail_b):
            hpb = hp % 2
            pend = []
            denav_q = []
            uu = 0
            units = []
            for qt in range(4):
                for (ki, pat) in visible_kis(role, qt):
                    units.append((qt, ki, pat))
            nu = len(units)
            assert nu == 72, f"role {role} hp {hp}: {nu} units"
            qlast = {}
            qn = {}
            for i, (qt, ki, pat) in enumerate(units):
                qlast[qt] = i
                qn[qt] = qn.get(qt, 0) + 1
            zq_of = {}
            for i, (qt, ki, pat) in enumerate(units):
                pi = i % 2
                for it in list(pend):
                    if i >= it[0]:
                        tail_b(it[1], it[2], hp, it[3])
                        pend.remove(it)
                zq = hp * 4 + qt
                qpar = zq % 2
                first_of_q = (i == 0 or units[i - 1][0] != qt)
                ki_first = first_of_q
                ki_last = (i == qlast[qt])
                if i >= 2:
                    mk.wait("tensor", "sc", E5[(hp, role, i - 2)])
                if i == 0:
                    mk.wait("tensor", "ve", v_ready)
                    mk.wait("tensor", "gg", mk.g_masks5)
                s1 = PSB[0 + pi]; s2 = PSB[2 + pi]
                stop_s = (pat is None) or (not MASK5)
                mk.op("tensor", "matmul", s1[:, :],
                      KB[0:64, ki * 128:(ki + 1) * 128],
                      QB[0:64, qt * 512:(qt + 1) * 512],
                      start=True, stop=stop_s, tile_position=(0, 0), ctr="tt")
                mk.op("tensor", "matmul", s2[:, :],
                      KB[64:128, ki * 128:(ki + 1) * 128],
                      QB[64:128, qt * 512:(qt + 1) * 512],
                      start=True, stop=stop_s, tile_position=(64, 0), ctr="tt")
                if pat is not None and MASK5:
                    msl = masksL1[:, pat * 512:(pat + 1) * 512]
                    mk.op("tensor", "matmul", s1[:, :], identb[:, :], msl,
                          start=False, stop=True, ctr="tt")
                    mk.op("tensor", "matmul", s2[:, :], identb[:, :], msl,
                          start=False, stop=True, ctr="tt")
                elif EQNOPS:
                    mk.nop_inc("tensor", "tt", 2)
                ts = mk.cnt["tt"]
                if denav_q and UNITLVL >= 2:
                    emit_denav5(denav_q.pop(0), hp, role, E5, Ta5, st5, tail_a)
                elif denav_q:
                    denav_q.pop(0)
                mk.wait("scalar", "tt", ts)
                if i >= 2 and UNITLVL >= 2:
                    mk.wait("scalar", "tt", Ta5[(hp, role, i - 2)])
                mk.op("scalar", "activation", EXT[0 + pi][:, :], s1[:, :],
                      AF.Exp, bias=0.0, scale=0.125, ctr="sc")
                mk.op("scalar", "activation", EXT[2 + pi][:, :], s2[:, :],
                      AF.Exp, bias=0.0, scale=0.125, ctr="sc")
                E5[(hp, role, i)] = mk.cnt["sc"]
                denav_q.append((i, zq, qpar, ki_first, ki_last, pi, ki, qt))
            if UNITLVL >= 2:
                emit_denav5(denav_q.pop(0), hp, role, E5, Ta5, st5, tail_a)
            else:
                denav_q.pop(0)
            assert not denav_q
            for it in list(pend):
                tail_b(it[1], it[2], hp, it[3])
                pend.remove(it)
            E5[(hp, "last")] = E5[(hp, role, nu - 1)]
            Ta5[(hp, "last")] = Ta5.get((hp, role, nu - 1), mk.cnt["tt"])

        def emit_denav5(item, hp, role, E5, Ta5, st5, tail_a):
            i, zq, qpar, ki_first, ki_last, pi, ki, qt = item
            DEN = PSB[6 + qpar]
            Z = PSB[4 + qpar]
            mk.wait("tensor", "sc", E5[(hp, role, i)])
            if ki_first:
                if zq >= 2 and UNITLVL >= 3:
                    mk.wait("tensor", "sc", st5["rbc_evac"][zq - 2])
                    mk.wait("tensor", "ve", st5["z_evac"][zq - 2])
                mk.op("tensor", "matmul", DEN[:, :], zerob[:, :], EXT[0 + pi][:, :],
                      start=True, stop=False, ctr="tt")
                mk.op("tensor", "matmul", Z[:, :], zerob[:, :], EXT[0 + pi][:, :],
                      start=True, stop=False, ctr="tt")
            elif EQNOPS:
                mk.nop_inc("tensor", "tt", 2)
            mk.op("tensor", "matmul", DEN[0:1, :], onesb[:, 0:1], EXT[0 + pi][:, :],
                  start=False, stop=False, tile_position=(0, 0),
                  skip_group_check=True, ctr="tt")
            mk.op("tensor", "matmul", DEN[64:65, :], onesb[:, 0:1], EXT[2 + pi][:, :],
                  start=False, stop=False, tile_position=(0, 64),
                  skip_group_check=True, ctr="tt")
            vsl0 = VTM[:, ki * 256 + (hp % 2) * 128: ki * 256 + (hp % 2) * 128 + 64]
            vsl1 = VTM[:, ki * 256 + (hp % 2) * 128 + 64: ki * 256 + (hp % 2) * 128 + 128]
            mk.op("tensor", "matmul", Z[0:64, :], vsl0, EXT[0 + pi][:, :],
                  start=False, stop=False, tile_position=(0, 0),
                  skip_group_check=True, ctr="tt")
            mk.op("tensor", "matmul", Z[64:128, :], vsl1, EXT[2 + pi][:, :],
                  start=False, stop=False, tile_position=(0, 64),
                  skip_group_check=True, ctr="tt")
            if ki_last:
                mk.op("tensor", "matmul", DEN[:, :], zerob[:, :], EXT[0 + pi][:, :],
                      start=False, stop=True, ctr="tt")
                mk.op("tensor", "matmul", Z[:, :], zerob[:, :], EXT[0 + pi][:, :],
                      start=False, stop=True, ctr="tt")
            elif EQNOPS:
                mk.nop_inc("tensor", "tt", 2)
            Ta5[(hp, role, i)] = mk.cnt["tt"]
            if ki_last and UNITLVL >= 3:
                tail_a(zq, qpar, Ta5[(hp, role, i)])
                mk.p5_tailb(zq, qpar, hp, qt)

        if "p5" in phases:
            pass

        def phase6():
            zall = reg(0, NHP * TOWN, BF16)
            ows = reg(52, CC * CC * 128, BF16)
            sst = [reg(32, 512), reg(34, 512)]
            for hp in range(NHP):
                mk.wait("gpsimd", "hw", mk.p5_zspill[hp])
            mk.dma("gpsimd", ows[:, :].rearrange("p (z d f) -> p z d f", z=CC, d=CC),
                   din["outw1"].ap().rearrange("z d p f -> p z d f"))
            g_ow = mk.cnt["gw"]
            mk.wait("gpsimd", "ve", mk.p5_done["ve"])
            mk.wait("gpsimd", "hw", mk.dq_done)
            g_z = mk.dma("gpsimd", zall[:, :].rearrange("p (h t) -> p h t", h=NHP),
                         d_z1.ap().rearrange("h p t -> p h t"))
            ev_s = {}
            s1dma = {}
            for dc in range(CC):
                for t4 in range(4):
                    j = dc * 4 + t4
                    pb = PSB[j % 2]
                    if j == 0:
                        mk.wait("tensor", "gw", g_ow)
                        mk.wait("tensor", "gw", g_z)
                        mk.wait("tensor", "sc", mk.cnt["sc"])
                        mk.wait("tensor", "ve", mk.cnt["ve"])
                    if j >= 2:
                        mk.wait("tensor", "sc", ev_s[j - 2])
                    for zc in range(CC):
                        mk.op("tensor", "matmul", pb[:, :],
                              ows[:, (zc * CC + dc) * 128:(zc * CC + dc + 1) * 128],
                              zall[:, zc * TOWN + t4 * 512: zc * TOWN + (t4 + 1) * 512],
                              start=(zc == 0), stop=(zc == CC - 1), ctr="tt")
                    mk.wait("scalar", "tt", mk.cnt["tt"])
                    if j >= 2:
                        mk.wait("scalar", "hw", s1dma[j - 2])
                    mk.op("scalar", "activation", sst[j % 2][:, :], pb[:, :], AF.Identity,
                          bias=bouts[:, dc:dc + 1], scale=1.0, ctr="sc")
                    ev_s[j] = mk.cnt["sc"]
                    mk.wait("sync", "sc", ev_s[j])
                    s1dma[j] = mk.dma("sync", d_s1.ap()[dc][:, t4 * 512:(t4 + 1) * 512],
                                      sst[j % 2][:, :])
            outproj_t = mk.cnt["tt"]

            w1ring = [reg(0, CC * 256, BF16), reg(4, CC * 256, BF16), reg(8, CC * 256, BF16)]
            w2d = [reg(16, DFF, BF16), reg(24, DFF, BF16)]
            hins = reg(36, CC * 512, BF16)
            hL = reg(44, FT * 512, BF16)
            x1t = [reg(76, 512), reg(78, 512)]
            s1t = [reg(80, 512), reg(82, 512)]
            mub6 = reg(84, 512); rb6 = reg(86, 512); murb6 = reg(88, 512)
            scr16 = reg(90, 512); scr26 = reg(92, 512); st26 = reg(94, 1024)
            gstg = reg(98, 512); t2 = reg(100, 512)
            x2s = reg(102, 512, BF16)
            ln_s = {}
            w2g = {}
            fc2mm = {}
            comb_v = {}
            odma = {}
            w1g = {}
            w1_lastmm = {}

            def load_w1slot(t4, k):
                gk = t4 * 16 + k
                if gk >= 3:
                    mk.wait("gpsimd", "tt", w1_lastmm[gk - 3])
                else:
                    mk.wait("gpsimd", "tt", outproj_t)
                    mk.wait("gpsimd", "hw", mk.dq_done)
                w1g[gk] = mk.dma("gpsimd",
                    w1ring[gk % 3][:, :].rearrange("p (c f) -> p c f", c=CC),
                    d_w1bf.ap().rearrange("c p f -> p c f")[:, :, k * 256:(k + 1) * 256])

            for t4 in range(4):
                SUMB, SQB = PSB[0], PSB[1]
                for cc in range(CC):
                    xs = XNo[:, cc * TOWN + t4 * 512: cc * TOWN + (t4 + 1) * 512]
                    k6 = t4 * CC + cc
                    if k6 >= 1:
                        mk.wait("vector", "tt", mm6_sq[k6 - 1])
                    else:
                        mk.wait("vector", "tt", mk.cnt["tt"])
                    mk.op("vector", "tensor_tensor", x2s[:, :], xs, xs, ALU.mult, ctr="ve")
                    v_sq = mk.cnt["ve"]
                    if cc == 0:
                        mk.wait("tensor", "sc", ln_s.get(t4 - 1, mk.cnt["sc"]))
                        if t4 == 0:
                            mk.wait("tensor", "ve", mk.cnt["ve"])
                    mk.wait("tensor", "ve", v_sq)
                    mk.op("tensor", "matmul", SUMB[0:1, :], onesb[:, 0:1], xs,
                          start=(cc == 0), stop=(cc == CC - 1), ctr="tt")
                    mk.op("tensor", "matmul", SQB[0:1, :], onesb[:, 0:1], x2s[:, :],
                          start=(cc == 0), stop=(cc == CC - 1), ctr="tt")
                    mm6_sq[k6] = mk.cnt["tt"]
                t_st = mk.cnt["tt"]
                mk.wait("scalar", "tt", t_st)
                mk.op("scalar", "copy", st26[0:1, 0:512], SUMB[0:1, :], ctr="sc")
                mk.op("scalar", "copy", st26[0:1, 512:1024], SQB[0:1, :], ctr="sc")
                s_ev = mk.cnt["sc"]
                mk.wait("tensor", "sc", s_ev)
                mk.op("tensor", "matmul", SUMB[:, :], ones1f[:, :], st26[0:1, 0:512],
                      start=True, stop=True, ctr="tt")
                mk.op("tensor", "matmul", SQB[:, :], ones1f[:, :], st26[0:1, 512:1024],
                      start=True, stop=True, ctr="tt")
                mk.wait("scalar", "tt", mk.cnt["tt"])
                mk.op("scalar", "mul", mub6[:, :], SUMB[:, :], 1.0 / D, ctr="sc")
                mk.op("scalar", "mul", scr16[:, :], SQB[:, :], 1.0 / D, ctr="sc")
                mk.wait("vector", "sc", mk.cnt["sc"])
                mk.op("vector", "tensor_tensor", scr26[:, :], mub6[:, :], mub6[:, :], ALU.mult, ctr="ve")
                mk.op("vector", "tensor_sub", scr16[:, :], scr16[:, :], scr26[:, :], ctr="ve")
                mk.op("vector", "tensor_single_scalar", scr16[:, :], scr16[:, :], EPS, ALU.add, ctr="ve")
                mk.op("vector", "reciprocal", scr26[:, :], scr16[:, :], ctr="ve")
                mk.wait("scalar", "ve", mk.cnt["ve"])
                mk.op("scalar", "activation", rb6[:, :], scr26[:, :], AF.Sqrt, ctr="sc")
                ln_s[t4] = mk.cnt["sc"]
                mk.wait("vector", "sc", ln_s[t4])
                mk.op("vector", "tensor_tensor", murb6[:, :], mub6[:, :], rb6[:, :], ALU.mult, ctr="ve")
                for cc in range(CC):
                    xs = XNo[:, cc * TOWN + t4 * 512: cc * TOWN + (t4 + 1) * 512]
                    dst = hins[:, cc * 512:(cc + 1) * 512]
                    if t4 == 0 and cc == 0:
                        mk.wait("vector", "hw", mk.dq_done)
                    if t4 > 0 and cc == 0:
                        mk.wait("vector", "tt", fc1last_t[t4 - 1])
                    mk.op("vector", "tensor_tensor", dst, xs, rb6[:, :], ALU.mult, ctr="ve")
                    mk.op("vector", "tensor_sub", dst, dst, murb6[:, :], ctr="ve")
                    v_hin = mk.cnt["ve"]
                    mk.wait("scalar", "ve", v_hin)
                    mk.op("scalar", "activation", dst, dst, AF.Identity,
                          bias=lbets[:, cc:cc + 1], scale=lgams[:, cc:cc + 1], ctr="sc")
                hin_s = mk.cnt["sc"]
                load_w1slot(t4, 0)
                load_w1slot(t4, 1)
                relu6 = {}
                for ft in range(FT):
                    k = ft // 2
                    gk = t4 * 16 + k
                    if ft % 2 == 0 and k + 2 < 16:
                        load_w1slot(t4, k + 2)
                    pb = PSB[2 + ft % 2]
                    if ft == 0:
                        mk.wait("tensor", "sc", hin_s)
                        mk.wait("tensor", "tt", outproj_t)
                        if t4 > 0:
                            mk.wait("tensor", "ve", comb_v[t4 - 1])
                    if ft >= 2:
                        mk.wait("tensor", "sc", relu6[ft - 2])
                    mk.wait("tensor", "gw", w1g[gk])
                    for cc in range(CC):
                        mk.op("tensor", "matmul", pb[:, :],
                              w1ring[gk % 3][:, cc * 256 + (ft % 2) * 128:
                                             cc * 256 + (ft % 2) * 128 + 128],
                              hins[:, cc * 512:(cc + 1) * 512],
                              start=(cc == 0), stop=(cc == CC - 1), ctr="tt")
                    if ft % 2 == 1:
                        w1_lastmm[gk] = mk.cnt["tt"]
                    mk.wait("scalar", "tt", mk.cnt["tt"])
                    mk.op("scalar", "activation", hL[:, ft * 512:(ft + 1) * 512],
                          pb[:, :], AF.Silu,
                          bias=fc1bs[:, ft:ft + 1], scale=1.0, ctr="sc")
                    relu6[ft] = mk.cnt["sc"]
                fc1last_t[t4] = mk.cnt["tt"]
                h6_s = mk.cnt["sc"]
                for dc in range(CC):
                    wb = w2d[dc % 2]
                    if (t4, dc) == (0, 0):
                        mk.wait("gpsimd", "hw", mk.dq_done)
                    prevmm = fc2mm.get((t4, dc - 2), fc2mm.get((t4 - 1, dc + 6), None))
                    if prevmm is not None:
                        mk.wait("gpsimd", "tt", prevmm)
                    w2g[(t4, dc)] = mk.dma("gpsimd", wb[:, :], d_w2bf.ap()[dc])
                    pb = PSB[6 + dc % 2]
                    mk.wait("tensor", "gw", w2g[(t4, dc)])
                    if dc == 0:
                        mk.wait("tensor", "sc", h6_s)
                        if t4 == 0:
                            mk.wait("tensor", "ve", mk.cnt["ve"])
                        else:
                            mk.wait("tensor", "ve", comb6_v[(t4 - 1, CC - 1)])
                    elif dc >= 2:
                        mk.wait("tensor", "ve", comb6_v[(t4, dc - 2)])
                    for fc in range(FT):
                        mk.op("tensor", "matmul", pb[:, :],
                              wb[:, fc * 128:(fc + 1) * 128],
                              hL[:, fc * 512:(fc + 1) * 512],
                              start=(fc == 0), stop=(fc == FT - 1), ctr="tt")
                    fc2mm[(t4, dc)] = mk.cnt["tt"]
                    prevod = odma.get((t4, dc - 2), odma.get((t4 - 1, dc + 6), None))
                    if prevod is not None:
                        mk.wait("sync", "hw", prevod)
                    hx1 = mk.dma("sync", x1t[dc % 2][:, :],
                                 d_x1.ap()[dc][:, t4 * 512:(t4 + 1) * 512])
                    hs1 = mk.dma("sync", s1t[dc % 2][:, :],
                                 d_s1.ap()[dc][:, t4 * 512:(t4 + 1) * 512])
                    mk.wait("vector", "tt", fc2mm[(t4, dc)])
                    mk.wait("vector", "hw", hs1)
                    if prevod is not None:
                        mk.wait("vector", "hw", prevod)
                    mk.op("vector", "tensor_scalar", t2[:, :], pb[:, :],
                          fc2bs[:, dc:dc + 1], 0.5, ALU.add, ALU.mult, ctr="ve")
                    mk.op("vector", "tensor_add", t2[:, :], t2[:, :],
                          XNo[:, dc * TOWN + t4 * 512: dc * TOWN + (t4 + 1) * 512], ctr="ve")
                    v_t2 = mk.cnt["ve"]
                    mk.wait("scalar", "ve", v_t2)
                    mk.op("scalar", "activation", gstg[:, :], t2[:, :], AF.Silu, ctr="sc")
                    mk.wait("vector", "sc", mk.cnt["sc"])
                    mk.op("vector", "tensor_tensor", t2[:, :], gstg[:, :],
                          s1t[dc % 2][:, :], ALU.mult, ctr="ve")
                    mk.op("vector", "tensor_add", t2[:, :], t2[:, :],
                          x1t[dc % 2][:, :], ctr="ve")
                    comb6_v[(t4, dc)] = mk.cnt["ve"]
                    comb_v[t4] = mk.cnt["ve"]
                    mk.wait("sync", "ve", comb6_v[(t4, dc)])
                    odma[(t4, dc)] = mk.dma("sync",
                        bass.AP(out_t, dc * 128 * TOWN + t4 * 512, [[TOWN, 128], [1, 512]]),
                        t2[:, :])
            mk.p6_odma = mk.cnt["hw"]

        mm6_sq = {}; comb6_v = {}; fc1last_t = {}
        if "p5" in phases:
            phase5()
        if "p6" in phases:
            phase6()


        if dbg:
            mk.wait("sync", "ve", mk.cnt["ve"])
            mk.wait("sync", "sc", mk.cnt["sc"])
            mk.wait("sync", "tt", mk.cnt["tt"])
            if "p4" not in phases:
                t_xn = dbg_out("t_xn", (128, CC * TFULL), BF16)
                for cc in range(CC):
                    mk.dma("sync", t_xn.ap()[:, cc * TFULL:cc * TFULL + TOWN],
                           XNo[:, cc * TOWN:(cc + 1) * TOWN])
                    mk.dma("sync", t_xn.ap()[:, cc * TFULL + TOWN:(cc + 1) * TFULL],
                           XNs[:, cc * TOWN:(cc + 1) * TOWN])
            if "p1" not in phases:
                t_cs = dbg_out("t_cs", (128, 2 * TFULL), BF16)
                mk.dma("sync", t_cs.ap()[:, 0:TFULL], csc[:, :])
                mk.dma("sync", t_cs.ap()[:, TFULL:2 * TFULL], css[:, :])
            t_lg = dbg_out("t_lg", (8, TOWN))
            mk.dma("sync", t_lg.ap(), d_lg.ap())
            if "p1" in phases:
                t_z0 = dbg_out("t_z0", (NHP, 128, TOWN), BF16)
                mk.dma("sync", t_z0.ap(), d_z0.ap())
            if "p2" in phases:
                t_s0 = dbg_out("t_s0", (CC, 128, TOWN))
                for j, dm in sdma.items():
                    mk.wait("sync", "hw", dm)
                mk.dma("sync", t_s0.ap(), d_s0.ap())
            if "p3" in phases:
                if "p4" not in phases:
                    t_mask = dbg_out("t_mask", (8, TOWN), BF16)
                    mk.dma("sync", t_mask.ap(), masksL1[0:8, 0:TOWN])
                    t_h = dbg_out("t_h", (128, TOWN), BF16)
                    mk.dma("sync", t_h.ap(), h_sl(0, 0, TOWN))
                t_y = dbg_out("t_y", (CC, 128, TOWN))
                mk.wait("sync", "gw", mk.cnt["gw"])
                mk.dma("sync", t_y.ap(), d_y.ap())
            if "p4" in phases:
                t_x1 = dbg_out("t_x1", (CC, 128, TOWN))
                mk.wait("sync", "ve", mk.cnt["ve"])
                mk.dma("sync", t_x1.ap(), d_x1.ap())
                t_ag = dbg_out("t_ag", (2, 128, CC * TOWN), BF16)
                mk.wait("sync", "cc", mk.cnt["cc"])
                mk.dma("sync", t_ag.ap(), cc_out.ap())
            if "p5" in phases:
                mk.wait("sync", "hw", mk.cnt["hw"])
                mk.wait("sync", "gw", mk.cnt["gw"])
                t_z1 = dbg_out("t_z1", (NHP, 128, TOWN), BF16)
                mk.dma("sync", t_z1.ap(), d_z1.ap())
            if "p6" in phases:
                t_s1 = dbg_out("t_s1", (CC, 128, TOWN))
                mk.dma("sync", t_s1.ap(), d_s1.ap())
                t_w1 = dbg_out("t_w1", (CC, 128, DFF), BF16)
                mk.dma("sync", t_w1.ap(), d_w1bf.ap())
                t_w2 = dbg_out("t_w2", (CC, 128, DFF), BF16)
                mk.dma("sync", t_w2.ap(), d_w2bf.ap())

        mk.em("sync", lambda E, s=mk.sem["hw"], v=mk.cnt["hw"]: E.wait_ge(s, v))
        mk.em("gpsimd", lambda E, s=mk.sem["gw"], v=mk.cnt["gw"]: E.wait_ge(s, v))

        mk.run_block(block)

    return nc, dbg_list

_CACHED = {}

def kernel(**inputs):
    if "nc" not in _CACHED:
        nc, _ = build_program(dbg=False,
                              phases=("p0", "p1", "p2", "p3", "p4", "p5", "p6"))
        _CACHED["nc"] = nc
    nc = _CACHED["nc"]
    I = {k: np.asarray(v) for k, v in inputs.items()}
    in_maps = [prep_core_inputs(I, c) for c in range(NCORES)]
    res = run_bass_kernel_spmd(nc, in_maps, core_ids=list(range(NCORES)))
    return assemble_output(res.results)
